# version 12
# baseline (speedup 1.0000x reference)
import numpy as np
import concourse.bass as bass
import concourse.mybir as mybir
from concourse.bass_utils import run_bass_kernel_spmd
from concourse.alu_op_type import AluOpType

F32 = mybir.dt.float32
BF16 = mybir.dt.bfloat16
AF = mybir.ActivationFunctionType
AX = mybir.AxisListType
OP = AluOpType

P = 128
T = 2048
C = 1024
NF = 64
KT = C // P
TC = T // 512
NTT = T // P
I = 2816
IKT = I // P
EPS_LN = 1e-5
EPS_MM = 1e-6
PI = float(np.pi)
C1H = float(np.float32(2.0 * np.pi)) / 2.0
C2H = float(np.float32(2.0 * np.pi - np.float64(np.float32(2.0 * np.pi)))) / 2.0
MIDS = [0.25, 0.75, 1.25, 1.75, 2.5, 3.5, 5.0]
NLS = [1, 2, 4, 8, 16, 32, 64, 128, 256, 512, 1024]
STOP_AFTER = None
DBG_SPILL = False
NO_QUANT = False


class _StopBuild(Exception):
    pass


class Prog:
    def __init__(self):
        self.script = []
        self.cnt = {n: 0 for n in ["qi", "qo", "t", "v", "s", "g"]}

    def now(self, sem):
        return self.cnt[sem]

    def emit(self, eng_name, body, waits=(), inc=None, inc_by=1):
        if getattr(self, "stopped", False):
            return
        if inc is not None:
            self.cnt[inc] += inc_by
        waits = tuple(waits)
        inc_t, n_inc = inc, inc_by

        def fn(eng, sems):
            for sem, val in waits:
                if val > 0:
                    eng.wait_ge(sems[sem], val)
            inst = body(eng) if body is not None else None
            if inc_t is not None:
                if inst is None:
                    inst = eng.nop()
                inst.then_inc(sems[inc_t], n_inc)
        self.script.append((eng_name, fn))


def build_program():
    nc = bass.Bass("TRN2", target_bir_lowering=False, debug=False)
    pr = Prog()
    E, NOW = pr.emit, pr.now

    def _ckpt(name, dumps):
        if STOP_AFTER != name:
            return
        barrier = [(s, NOW(s)) for s in ("t", "v", "s", "qi", "qo", "g")]
        first = True
        for ap, row0 in dumps:
            E("gpsimd", (lambda a_, r_: (lambda eng: eng.dma_start(
                out=outT[r_:r_ + a_.shape[0], 0:a_.shape[1]], in_=a_)))(ap, row0),
              waits=(barrier if first else []), inc="qo", inc_by=16)
            first = False
        E("gpsimd", None, waits=[("qo", NOW("qo"))])
        pr.stopped = True

    def din(name, shape, dtype=F32):
        return nc.dram_tensor(name, shape, dtype, kind="ExternalInput").ap()

    xT = din("xT", [C, T])
    xprevT = din("xprevT", [C, T])
    attn_mk = din("attn_mk", [P, 16 * C])
    proj_mk = din("proj_mk", [P, KT * C])
    wqT_t = din("wqT_t", [P, KT * C])
    wfs_t = din("wfs_t", [P, KT * P])
    outw = din("outw", [P, C])
    wmod_t = din("wmod_t", [P, KT * 6 * C])
    wg_nat = din("wg_nat", [I, C])
    wu_nat = din("wu_nat", [I, C])
    wd_nat = din("wd_nat", [C, I])
    attb = din("attb", [P, 16])
    outb = din("outb", [P, KT])
    projb = din("projb", [P, KT])
    fbcol = din("fbcol", [NF, 1])
    idcol = din("idcol", [P, KT])
    c_col = din("c_col", [P, KT])
    evenm = din("evenm", [NF, 1])
    eye = din("eye", [P, P])
    selm16 = din("selm16", [16, T], BF16)
    selrowa = din("selrowa", [16, C], BF16)
    selrowb = din("selrowb", [16, C], BF16)

    outT = nc.dram_tensor("outT", [C, T], F32, kind="ExternalOutput").ap()

    spk = "ExternalOutput" if DBG_SPILL else "Internal"
    vspill = nc.dram_tensor("vspill", [C, T], F32, kind="Internal").ap()
    x2spill = nc.dram_tensor("x2spill", [C, T], F32, kind="Internal").ap()
    wgspill = nc.dram_tensor("wgspill", [IKT, KT, P, P], F32, kind=spk).ap()
    wuspill = nc.dram_tensor("wuspill", [IKT, KT, P, P], F32, kind=spk).ap()
    wdspill = nc.dram_tensor("wdspill", [KT, IKT, P, P], F32, kind=spk).ap()
    hqspill = nc.dram_tensor("hqspill", [6, TC, P, 4 * 512], F32, kind=spk).ap()

    guards = []

    def S(name, shape, dtype=F32):
        g = nc.sbuf_tensor(name, shape, dtype)
        t = g.__enter__()
        guards.append(g)
        return t

    AR1 = S("AR1", [P, 8 * T])
    AR2 = S("AR2", [P, 8 * T])
    SCR = S("SCR", [P, 6144])
    STRM = S("STRM", [P, 4096])
    C_wfs = S("C_wfs", [P, KT * P])
    C_outw = S("C_outw", [P, C])
    C_eye = S("C_eye", [P, P])
    C_wqf = S("C_wqf", [P, KT * NF])
    C_attb = S("C_attb", [P, 16])
    C_outb = S("C_outb", [P, KT])
    C_modv = S("C_modv", [P, 48])
    C_der = S("C_der", [P, 32])
    C_sil = S("C_sil", [P, KT])
    C_misc = S("C_misc", [P, 16])
    C_svec = S("C_svec", [P, 16])
    C_cvec = S("C_cvec", [P, 16])
    C_nf = S("C_nf", [NF, 8])
    C_selm16 = S("C_selm16", [16, T], BF16)
    C_selrowa = S("C_selrowa", [16, C], BF16)
    C_selrowb = S("C_selrowb", [16, C], BF16)
    C_ones16 = S("C_ones16", [16, P])
    MTM = S("MTM", [P, 128])
    MTT = S("MTT", [16, P])
    RREP = S("RREP", [P, T])
    MUREP = S("MUREP", [P, T])

    pg_ = nc.psum_tensor("pG", [P, T], F32); pG = pg_.__enter__(); guards.append(pg_)
    pa_ = nc.psum_tensor("pA", [P, 512], F32); pA = pa_.__enter__(); guards.append(pa_)
    pb_ = nc.psum_tensor("pB", [P, 512], F32); pB = pb_.__enter__(); guards.append(pb_)
    pt0_ = nc.psum_tensor("pT0", [P, P], F32); pT0 = pt0_.__enter__(); guards.append(pt0_)
    pt1_ = nc.psum_tensor("pT1", [P, P], F32); pT1 = pt1_.__enter__(); guards.append(pt1_)
    pAB = [pA, pB]
    pTT = [pT0, pT1]

    def ar1(slot, rows=P):
        return AR1[0:rows, slot * T:(slot + 1) * T]

    def ar2(slot, rows=P):
        return AR2[0:rows, slot * T:(slot + 1) * T]

    RHSM = SCR[0:16, 2048:4096]

    for dst, src in [(C_wfs, wfs_t), (C_outw, outw), (C_eye, eye),
                     (C_attb, attb), (C_outb, outb), (C_sil, c_col),
                     (C_selm16, selm16), (C_selrowa, selrowa), (C_selrowb, selrowb)]:
        E("sync", (lambda d, s_: (lambda eng: eng.dma_start(out=d[:], in_=s_)))(dst, src),
          inc="qi", inc_by=16)
    E("sync", lambda eng: eng.dma_start(out=C_misc[:, 0:8], in_=projb), inc="qi", inc_by=16)
    E("sync", lambda eng: eng.dma_start(out=C_svec[:, 0:8], in_=idcol), inc="qi", inc_by=16)
    E("sync", lambda eng: eng.dma_start(out=C_nf[:, 3:4], in_=fbcol), inc="qi", inc_by=16)
    E("sync", lambda eng: eng.dma_start(out=C_nf[:, 4:5], in_=evenm), inc="qi", inc_by=16)
    pre_dma = NOW("qi")

    def mk_consts(eng):
        eng.memset(C_ones16[:], 1.0)
        eng.memset(C_misc[:, 8:9], 1.0)
        eng.memset(C_misc[:, 11:12], -PI)
        eng.memset(C_misc[:, 12:13], PI)
        return None
    E("vector", mk_consts, inc="v")
    v_consts = NOW("v")

    E("scalar", lambda eng: eng.activation(C_sil[:], C_sil[:], AF.Silu),
      waits=[("qi", pre_dma)], inc="s")
    s_sil = NOW("s")

    for k in range(KT):
        sl = (k % 2) * 6 * C
        wl = [("t", NOW("t"))] if k >= 2 else []
        E("sync", (lambda kk, s_: (lambda eng: eng.dma_start(
            out=AR2[:, s_:s_ + 6 * C], in_=wmod_t[:, kk * 6 * C:(kk + 1) * 6 * C])))(k, sl),
          waits=wl, inc="qi", inc_by=16)
        dd = NOW("qi")

        def mk_mod(kk, s_):
            def body(eng):
                inst = None
                for m in range(48):
                    inst = eng.matmul(pG[:, m:m + 1], AR2[:, s_ + m * P:s_ + (m + 1) * P],
                                      C_sil[:, kk:kk + 1],
                                      start=(kk == 0 and m == 0),
                                      stop=(kk == KT - 1 and m == 47))
                return inst
            return body
        E("tensor", mk_mod(k, sl), waits=[("qi", dd), ("s", s_sil)], inc="t")
    t_mod = NOW("t")
    E("vector", lambda eng: eng.tensor_copy(C_modv[:], pG[:, 0:48]),
      waits=[("t", t_mod)], inc="v")
    v_mod = NOW("v")

    def mk_derived(eng):
        eng.tensor_scalar(C_der[:, 0:8], C_modv[:, 8:16], 1.0, None, OP.add)
        eng.tensor_scalar(C_der[:, 8:16], C_modv[:, 32:40], 1.0, None, OP.add)
        for src, dst in [(16, 16), (40, 24)]:
            g = C_modv[:, src:src + 8]
            eng.scalar_tensor_tensor(MTM[:, 0:8], g, -1.0, g, OP.mult, OP.max)
            eng.drain()
            eng.tensor_scalar(MTM[:, 0:8], MTM[:, 0:8], 0.25, None, OP.max)
            eng.tensor_single_scalar(MTM[:, 8:16], g, 0.0, OP.is_ge)
            eng.tensor_single_scalar(MTM[:, 16:24], g, 0.0, OP.is_lt)
            eng.drain()
            eng.tensor_tensor(MTM[:, 8:16], MTM[:, 8:16], MTM[:, 16:24], OP.subtract)
            eng.drain()
            eng.tensor_tensor(C_der[:, dst:dst + 8], MTM[:, 0:8], MTM[:, 8:16], OP.mult)
        eng.drain()
        return eng.tensor_tensor(C_misc[:, 0:8], C_misc[:, 0:8], C_der[:, 16:24], OP.mult)
    E("vector", mk_derived, waits=[("v", v_mod), ("qi", pre_dma)], inc="v")
    v_der = NOW("v")

    for k in range(KT):
        sl = (k % 2) * C
        wl = [("t", NOW("t"))] if k >= 2 else []
        E("sync", (lambda kk, s_: (lambda eng: eng.dma_start(
            out=STRM[:, s_:s_ + C], in_=wqT_t[:, kk * C:(kk + 1) * C])))(k, sl),
          waits=wl, inc="qi", inc_by=16)
        dd = NOW("qi")

        def mk_wqf(kk, s_):
            def body(eng):
                inst = None
                for m in range(KT):
                    inst = eng.matmul(pG[:, 1024 + m * NF:1024 + (m + 1) * NF],
                                      STRM[:, s_ + m * P:s_ + (m + 1) * P],
                                      C_wfs[:, kk * P:kk * P + NF],
                                      start=(kk == 0 and m == 0),
                                      stop=(kk == KT - 1 and m == KT - 1))
                return inst
            return body
        E("tensor", mk_wqf(k, sl), waits=[("qi", dd), ("v", v_mod)], inc="t")
    t_wqf = NOW("t")
    E("vector", lambda eng: eng.tensor_copy(C_wqf[:], pG[:, 1024:1024 + KT * NF]),
      waits=[("t", t_wqf)], inc="v")
    v_wqf = NOW("v")

    def mk_small(eng):
        inst = None
        for k in range(KT):
            inst = eng.matmul(pT0[0:NF, 0:1], C_wqf[:, k * NF:(k + 1) * NF],
                              C_der[:, k:k + 1], start=(k == 0), stop=False)
        for k in range(KT):
            inst = eng.matmul(pT0[0:NF, 1:2], C_wqf[:, k * NF:(k + 1) * NF],
                              C_modv[:, k:k + 1], start=False, stop=False)
        for k in range(KT):
            inst = eng.matmul(pT0[0:NF, 1:2], C_wfs[:, k * P:k * P + NF],
                              C_attb[:, k:k + 1], start=False, stop=False)
        for k in range(KT):
            inst = eng.matmul(pT0[0:NF, 2:3], C_wfs[:, k * P:k * P + NF],
                              C_svec[:, k:k + 1], start=False,
                              stop=(k == KT - 1))
        return inst
    E("tensor", mk_small, waits=[("v", v_der), ("v", v_wqf)], inc="t")
    t_small = NOW("t")

    def mk_nf(eng):
        eng.tensor_copy(C_nf[:, 1:2], pT0[0:NF, 0:1])
        eng.tensor_copy(C_nf[:, 2:3], pT0[0:NF, 1:2])
        eng.tensor_copy(C_nf[:, 0:1], pT0[0:NF, 2:3])
        eng.drain()
        eng.tensor_tensor(C_nf[:, 5:6], C_nf[:, 0:1], C_nf[:, 2:3], OP.subtract)
        eng.drain()
        eng.tensor_tensor(C_nf[:, 5:6], C_nf[:, 5:6], C_nf[:, 4:5], OP.mult)
        eng.tensor_tensor(C_nf[:, 6:7], C_nf[:, 2:3], C_nf[:, 3:4], OP.add)
        eng.drain()
        eng.tensor_scalar(C_nf[:, 7:8], C_nf[:, 6:7], PI / 2, None, OP.add)
        eng.drain()
        return eng.tensor_scalar(C_misc[0:NF, 10:11], C_nf[:, 3:4], PI / 2, None, OP.add)
    E("vector", mk_nf, waits=[("t", t_small), ("v", v_consts)], inc="v")
    v_nf = NOW("v")
    _ckpt("prelude", [(C_modv[:], 0), (C_der[:], 128), (C_wqf[:], 256),
                      (C_nf[:, 0:8], 384), (C_sil[:], 512)])

    def tr_batch(srcs, wait_list, off=0):
        n = len(srcs)

        def body(eng):
            inst = None
            for i, s_ in enumerate(srcs):
                first = (i % 4 == 0)
                last = (i % 4 == 3) or (i == n - 1)
                inst = eng.matmul(pG[:, (off + i) * P:(off + i + 1) * P], s_, C_eye[:],
                                  is_transpose=True, start=first, stop=last)
            return inst
        E("tensor", body, waits=wait_list, inc="t")
        E("tensor", lambda eng: eng.matmul(pT0[0:1, 0:1], C_eye[:, 0:1], C_eye[:, 0:1],
                                           start=True, stop=True), inc="t")
        return NOW("t")

    def stats_to_tm(wait_v, srow):
        def body(eng):
            inst = None
            for c2 in range(NTT):
                first = (c2 % 4 == 0)
                last = (c2 % 4 == 3)
                eng.matmul(pG[:, c2 * P:c2 * P + 1], SROW0[:, c2 * P:(c2 + 1) * P],
                           C_eye[0:1, 0:1], is_transpose=True,
                           start=first, stop=False)
                inst = eng.matmul(pG[:, c2 * P + 1:c2 * P + 2],
                                  SROW1[:, c2 * P:(c2 + 1) * P],
                                  C_eye[32:33, 32:33], is_transpose=True,
                                  start=False, stop=last)
            return inst
        E("tensor", body, waits=[("v", wait_v)], inc="t")
        E("tensor", lambda eng: eng.matmul(pT0[0:1, 0:1], C_eye[:, 0:1], C_eye[:, 0:1],
                                           start=True, stop=True), inc="t")
        tg = NOW("t")
        E("vector", lambda eng: eng.tensor_copy(
            MTM[:, 0:32].rearrange("p (c j) -> p c j", j=2),
            pG[:, 0:NTT * P].rearrange("p (c j) -> p c j", j=P)[:, :, 0:2]),
          waits=[("t", tg)], inc="v")
        return NOW("v")

    def ln_smallmath(wait_v):
        def body(eng):
            mu = MTM[:, 0:32:2]
            s2 = MTM[:, 1:32:2]
            eng.drain()
            eng.tensor_scalar(mu, mu, 1.0 / C, None, OP.mult)
            eng.tensor_scalar(s2, s2, 1.0 / C, None, OP.mult)
            eng.drain()
            eng.tensor_tensor(MTM[:, 32:48], mu, mu, OP.mult)
            eng.drain()
            eng.tensor_tensor(s2, s2, MTM[:, 32:48], OP.subtract)
            eng.drain()
            return eng.tensor_scalar(s2, s2, EPS_LN, None, OP.add)
        E("vector", body, waits=[("v", wait_v)], inc="v")
        v1 = NOW("v")
        E("scalar", lambda eng: eng.activation(MTM[:, 32:48], MTM[:, 1:32:2], AF.Sqrt),
          waits=[("v", v1)], inc="s")
        s1 = NOW("s")

        def body2(eng):
            r0 = MTM[:, 48:64]
            eng.reciprocal(r0, MTM[:, 32:48])
            eng.drain()
            eng.tensor_tensor(MTM[:, 64:80], r0, r0, OP.mult)
            eng.drain()
            eng.tensor_tensor(MTM[:, 64:80], MTM[:, 64:80], MTM[:, 1:32:2], OP.mult)
            eng.drain()
            eng.tensor_scalar(MTM[:, 64:80], MTM[:, 64:80], -0.5, None, OP.mult)
            eng.drain()
            eng.tensor_scalar(MTM[:, 64:80], MTM[:, 64:80], 1.5, None, OP.add)
            eng.drain()
            eng.tensor_tensor(MTM[:, 32:48], r0, MTM[:, 64:80], OP.mult)
            eng.drain()
            eng.tensor_tensor(MTM[:, 48:64], MTM[:, 0:32:2], MTM[:, 32:48], OP.mult)
            eng.drain()
            return eng.tensor_scalar(MTM[:, 48:64], MTM[:, 48:64], -1.0, None, OP.mult)
        E("vector", body2, waits=[("s", s1)], inc="v")
        return NOW("v")

    def replicate16(src_cols, wait_v, dest):
        E("tensor", (lambda sc: (lambda eng: eng.matmul(
            pT0[0:16, :], MTM[:, sc:sc + 16], C_eye[:],
            is_transpose=True, start=True, stop=True)))(src_cols),
          waits=[("v", wait_v)], inc="t")
        E("tensor", lambda eng: eng.matmul(pG[0:1, 0:1], C_eye[:, 0:1], C_eye[:, 0:1],
                                           start=True, stop=True), inc="t")
        t1 = NOW("t")

        def mk_rhs(eng):
            eng.tensor_copy(MTT[:], pT0[0:16, :])
            eng.drain()
            bc = MTT[:].unsqueeze(1).broadcast_to([16, NTT, P])
            return eng.tensor_tensor(RHSM.rearrange("k (c j) -> k c j", j=P),
                                     C_selm16[:].rearrange("k (c j) -> k c j", j=P),
                                     bc, OP.mult)
        E("vector", mk_rhs, waits=[("t", t1)], inc="v")
        v2 = NOW("v")

        def mk_rep(eng):
            inst = None
            for c in range(TC):
                inst = eng.matmul(pG[:, c * 512:(c + 1) * 512], C_ones16[:],
                                  RHSM[:, c * 512:(c + 1) * 512], start=True, stop=True)
            return inst
        E("tensor", mk_rep, waits=[("v", v2)], inc="t")
        ta = NOW("t")
        E("scalar", (lambda d: (lambda eng: (eng.activation(d[:], pG[:, 0:T], AF.Copy),
                                              eng.drain())[1]))(dest),
          waits=[("t", ta)], inc="s")
        E("vector", None, waits=[("s", NOW("s"))], inc="v")
        return NOW("v")

    def quant_tile(eng, src, dst, w, scratch, smalls):
        if NO_QUANT:
            if dst is not src:
                return eng.tensor_copy(dst, src)
            return eng.engine_nop()
        nb = w // 16
        nbw = max(nb, NF)
        bm = smalls[:, 0:nbw]
        inv = smalls[:, nbw:2 * nbw]
        eng.tensor_reduce(bm[:, 0:nb].unsqueeze(2),
                          src.rearrange("p (b i) -> p b i", i=16),
                          AX.X, OP.max, apply_absolute_value=True)
        if nb < nbw:
            eng.drain()
        eng.tensor_scalar(bm, bm, float(np.float32(1.0) / np.float32(6.0)), None, OP.mult)
        eng.tensor_scalar(bm, bm, 1e-12, None, OP.max)
        eng.reciprocal(inv, bm)
        eng.tensor_scalar(bm, bm, 0.5, None, OP.mult)
        inst = None
        for o in range(0, w, 512):
            ww = min(512, w - o)
            nbo, nbn = o // 16, ww // 16
            s_ = src[:, o:o + ww]
            d_ = dst[:, o:o + ww]
            a_ = scratch[:, 0:ww]
            g1 = scratch[:, 512:512 + ww]
            g2 = scratch[:, 1024:1024 + ww]
            tmp = scratch[:, 1536:1536 + ww]
            invb = inv[:, nbo:nbo + nbn].unsqueeze(2).broadcast_to([P, nbn, 16])
            bmb = bm[:, nbo:nbo + nbn].unsqueeze(2).broadcast_to([P, nbn, 16])
            eng.scalar_tensor_tensor(a_, s_, -1.0, s_, OP.mult, OP.max)
            eng.tensor_tensor(a_.rearrange("p (b i) -> p b i", i=16),
                              a_.rearrange("p (b i) -> p b i", i=16), invb, OP.mult)
            eng.tensor_single_scalar(g1, a_, MIDS[0], OP.is_gt)
            for mid in MIDS[1:4]:
                eng.tensor_single_scalar(tmp, a_, mid, OP.is_gt)
                eng.tensor_tensor(g1, g1, tmp, OP.add)
            eng.tensor_single_scalar(g2, a_, MIDS[4], OP.is_gt)
            eng.tensor_single_scalar(tmp, a_, MIDS[5], OP.is_gt)
            eng.tensor_tensor(g2, g2, tmp, OP.add)
            eng.tensor_single_scalar(tmp, a_, MIDS[6], OP.is_gt)
            eng.scalar_tensor_tensor(g2, tmp, 2.0, g2, OP.mult, OP.add)
            eng.scalar_tensor_tensor(g1, g2, 2.0, g1, OP.mult, OP.add)
            eng.tensor_tensor(g1.rearrange("p (b i) -> p b i", i=16),
                              g1.rearrange("p (b i) -> p b i", i=16), bmb, OP.mult)
            eng.tensor_single_scalar(tmp, s_, 0.0, OP.is_ge)
            eng.tensor_single_scalar(g2, s_, 0.0, OP.is_lt)
            eng.tensor_tensor(tmp, tmp, g2, OP.subtract)
            inst = eng.tensor_tensor(d_, g1, tmp, OP.mult)
        return inst

    def quant_chunks(src, w, regs, smalls, wl_first, state, sign_dve=False):
        a_r, tmp_r, g2_r, g1a, g1b, w1_r = regs
        nb = w // 16
        nbw = max(nb, NF)
        bm = smalls[:, 0:nbw]
        inv = smalls[:, nbw:2 * nbw]
        nchunks = (w + 511) // 512
        if NO_QUANT:
            def emit_nop(ci):
                if ci == 0:
                    E("vector", None, waits=wl_first, inc="v")
                return ("v", NOW("v"))
            return nchunks, emit_nop

        def emit_chunk(ci):
            o = ci * 512
            ww = min(512, w - o)
            nbo, nbn = o // 16, ww // 16
            s_ = src[:, o:o + ww]
            g1 = (g1a if ci % 2 == 0 else g1b)[:, 0:ww]
            a2 = a_r[:, 0:ww]
            tmp = tmp_r[:, 0:ww]
            g2 = g2_r[:, 0:ww]
            invb = inv[:, nbo:nbo + nbn].unsqueeze(2).broadcast_to([P, nbn, 16])
            bmb = bm[:, nbo:nbo + nbn].unsqueeze(2).broadcast_to([P, nbn, 16])
            par = "A" if ci % 2 == 0 else "B"

            def dve_body(first, s2, a3, t3, g23, g13, invb_):
                def body(eng):
                    if first:
                        eng.tensor_reduce(bm[:, 0:nb].unsqueeze(2),
                                          src.rearrange("p (b i) -> p b i", i=16),
                                          AX.X, OP.max, apply_absolute_value=True)
                        if nb < nbw:
                            eng.drain()
                        eng.tensor_scalar(bm, bm,
                                          float(np.float32(1.0) / np.float32(6.0)),
                                          None, OP.mult)
                        eng.tensor_scalar(bm, bm, 1e-12, None, OP.max)
                        eng.reciprocal(inv, bm)
                        eng.tensor_scalar(bm, bm, 0.5, None, OP.mult)
                    eng.scalar_tensor_tensor(a3, s2, -1.0, s2, OP.mult, OP.max)
                    eng.tensor_tensor(a3.rearrange("p (b i) -> p b i", i=16),
                                      a3.rearrange("p (b i) -> p b i", i=16),
                                      invb_, OP.mult)
                    eng.tensor_single_scalar(g13, a3, MIDS[0], OP.is_gt)
                    for mid in MIDS[1:4]:
                        eng.tensor_single_scalar(t3, a3, mid, OP.is_gt)
                        eng.tensor_tensor(g13, g13, t3, OP.add)
                    eng.tensor_single_scalar(g23, a3, MIDS[4], OP.is_gt)
                    eng.tensor_single_scalar(t3, a3, MIDS[5], OP.is_gt)
                    eng.tensor_tensor(g23, g23, t3, OP.add)
                    eng.tensor_single_scalar(t3, a3, MIDS[6], OP.is_gt)
                    eng.scalar_tensor_tensor(g23, t3, 2.0, g23, OP.mult, OP.add)
                    inst = eng.scalar_tensor_tensor(g13, g23, 2.0, g13, OP.mult, OP.add)
                    if sign_dve:
                        eng.tensor_single_scalar(t3, s2, 0.0, OP.is_ge)
                        eng.tensor_single_scalar(g23, s2, 0.0, OP.is_lt)
                        inst = eng.tensor_tensor(s2, t3, g23, OP.subtract)
                    return inst
                return body
            wl = list(wl_first) if ci == 0 else []
            gk = state.get("last") if ci == 0 else state.get(par)
            if gk is not None:
                wl.append(("g", gk))
            E("vector", dve_body(ci == 0, s_, a2, tmp, g2, g1, invb), waits=wl, inc="v")
            v_d = NOW("v")
            wl_g = [("v", v_d)]
            if not sign_dve:
                E("scalar", (lambda s2: (lambda eng: (eng.activation(s2, s2, AF.Sign),
                                                      eng.drain())[1]))(s_),
                  waits=[("v", v_d), ("s", NOW("s"))], inc="s")
                wl_g.append(("s", NOW("s")))

            def gps_body(s2, g13, bmb_, ww_):
                def body(eng):
                    eng.tensor_tensor(w1_r[:, 0:ww_].rearrange("p (b i) -> p b i", i=16),
                                      g13.rearrange("p (b i) -> p b i", i=16),
                                      bmb_, OP.mult)
                    return eng.tensor_tensor(s2, s2, w1_r[:, 0:ww_], OP.mult)
                return body
            E("gpsimd", gps_body(s_, g1, bmb, ww), waits=wl_g, inc="g")
            state[par] = NOW("g")
            state["last"] = NOW("g")
            return ("g", NOW("g"))
        return nchunks, emit_chunk

    def quant_emit(src, w, regs, smalls, wl_first, state, sign_dve=False):
        nchunks, emit_chunk = quant_chunks(src, w, regs, smalls, wl_first, state,
                                           sign_dve)
        ret = None
        for ci in range(nchunks):
            ret = emit_chunk(ci)
        return ret

    for k in range(KT):
        E("sync", (lambda kk: (lambda eng: eng.dma_start(
            out=ar1(kk), in_=xT[kk * P:(kk + 1) * P, :])))(k), inc="qi", inc_by=16)
    x_loaded = NOW("qi")

    tsq1_hist = []
    for k in range(KT):
        sb1 = (k % 2) * T

        def mk_sum(kk):
            def body(eng):
                inst = None
                for c in range(TC):
                    inst = eng.matmul(pG[0:1, c * 512:(c + 1) * 512], C_misc[:, 8:9],
                                      ar1(kk)[:, c * 512:(c + 1) * 512],
                                      start=(kk == 0), stop=(kk == KT - 1))
                return inst
            return body
        E("tensor", mk_sum(k), waits=[("qi", x_loaded), ("v", v_nf)], inc="t")
        wl_sq = [("qi", x_loaded), ("s", NOW("s"))]
        if k >= 2:
            wl_sq.append(("t", tsq1_hist[k - 2]))
        E("scalar", (lambda kk, s_: (lambda eng: eng.activation(
            SCR[:, s_:s_ + T], ar1(kk), AF.Square)))(k, sb1),
          waits=wl_sq, inc="s")
        s_h = NOW("s")

        def mk_sumsq(kk, s_):
            def body(eng):
                inst = None
                for c in range(TC):
                    inst = eng.matmul(pG[32:33, c * 512:(c + 1) * 512], C_misc[:, 8:9],
                                      SCR[:, s_ + c * 512:s_ + (c + 1) * 512],
                                      start=(kk == 0), stop=(kk == KT - 1))
                return inst
            return body
        E("tensor", mk_sumsq(k, sb1), waits=[("s", s_h)], inc="t")
        tsq1_hist.append(NOW("t"))
        E("scalar", (lambda kk: (lambda eng: eng.activation(
            ar1(kk), ar1(kk), AF.Copy, scale=C_der[:, kk:kk + 1])))(k),
          waits=[("t", NOW("t"))], inc="s")
    t_stats = NOW("t")
    s_xsc = NOW("s")

    SROW0 = SCR[0:1, 4096:6144]
    SROW1 = SCR[32:33, 4096:6144]
    def mk_srow(eng):
        eng.tensor_copy(SROW0, pG[0:1, :])
        return eng.tensor_copy(SROW1, pG[32:33, :])
    E("vector", mk_srow, waits=[("t", t_stats)], inc="v")
    v_tm = stats_to_tm(NOW("v"), None)
    v_ln = ln_smallmath(v_tm)
    v_r1 = replicate16(32, v_ln, RREP)
    v_r2 = replicate16(48, v_r1, MUREP)
    _ckpt("reps", [(RREP[:], 0), (MUREP[:], 128), (MTM[:], 256)])

    tq_hist = []
    ev_hist = []
    vvec_hist = []
    qo_hist = {}
    for m in range(16):
        sl = (m % 2) * C
        wl = [("t", tq_hist[m - 2])] if m >= 2 else [("v", v_r2)]
        E("sync", (lambda mm, s_: (lambda eng: eng.dma_start(
            out=STRM[:, s_:s_ + C], in_=attn_mk[:, mm * C:(mm + 1) * C])))(m, sl),
          waits=wl, inc="qi", inc_by=16)
        dd = NOW("qi")

        def mk_vecs(s_):
            def body(eng):
                inst = None
                for k in range(KT):
                    inst = eng.matmul(pT0[:, 4:5], STRM[:, s_ + k * P:s_ + (k + 1) * P],
                                      C_modv[:, k:k + 1], start=(k == 0), stop=False)
                for k in range(KT):
                    inst = eng.matmul(pT0[:, 5:6], STRM[:, s_ + k * P:s_ + (k + 1) * P],
                                      C_der[:, k:k + 1], start=False, stop=(k == KT - 1))
                return inst
            return body
        wl_vec = [("qi", dd), ("s", s_xsc)]
        if m >= 1:
            wl_vec.append(("v", vvec_hist[m - 1]))
        E("tensor", mk_vecs(sl), waits=wl_vec, inc="t")
        tv = NOW("t")
        E("vector", (lambda mm: (lambda eng: (
            eng.tensor_copy(C_svec[:, mm:mm + 1], pT0[:, 4:5]),
            eng.drain(),
            eng.tensor_tensor(C_svec[:, mm:mm + 1], C_svec[:, mm:mm + 1],
                              C_attb[:, mm:mm + 1], OP.add),
            eng.drain(),
            eng.tensor_copy(C_cvec[:, mm:mm + 1], pT0[:, 5:6]),
            eng.drain())[-2]))(m),
          waits=[("t", tv)], inc="v")
        vvec_hist.append(NOW("v"))
        for c in range(TC):
            ps = pAB[c % 2]
            j = m * TC + c

            def mk_qv(s_, cc, ps_):
                def body(eng):
                    inst = None
                    for k in range(KT):
                        inst = eng.matmul(ps_[:], STRM[:, s_ + k * P:s_ + (k + 1) * P],
                                          ar1(k)[:, cc * 512:(cc + 1) * 512],
                                          start=(k == 0), stop=(k == KT - 1))
                    return inst
                return body
            wl_mm = [("v", v_r2)]
            if j >= 2:
                wl_mm.append(("v", ev_hist[j - 2]))
            E("tensor", mk_qv(sl, c, ps), waits=wl_mm, inc="t")
            t_qv = NOW("t")

            def mk_evac(mm, cc, ps_):
                def body(eng):
                    cs = slice(cc * 512, (cc + 1) * 512)
                    if mm < KT:
                        dst = ar2(mm)[:, cs]
                    else:
                        vb = ((mm - KT) % 2) * T
                        dst = SCR[:, vb + cc * 512:vb + (cc + 1) * 512]
                    eng.tensor_tensor(dst, RREP[:, cs], ps_[:], OP.mult)
                    eng.scalar_tensor_tensor(dst, MUREP[:, cs], C_cvec[:, mm:mm + 1],
                                             dst, OP.mult, OP.add)
                    return eng.tensor_scalar(dst, dst, C_svec[:, mm:mm + 1], None, OP.add)
                return body
            wl_ev = [("t", t_qv), ("v", vvec_hist[m])]
            if c == 0 and m >= KT + 2:
                wl_ev.append(("qo", qo_hist[m - 2]))
            E("vector", mk_evac(m, c, ps), waits=wl_ev, inc="v")
            ev_hist.append(NOW("v"))
        tq_hist.append(NOW("t"))
        if m >= KT:
            vdone = NOW("v")
            E("gpsimd", (lambda mm: (lambda eng: eng.dma_start(
                out=vspill[(mm - KT) * P:(mm - KT + 1) * P, :],
                in_=SCR[:, ((mm - KT) % 2) * T:((mm - KT) % 2) * T + T])))(m),
              waits=[("v", vdone)], inc="qo", inc_by=16)
            qo_hist[m] = NOW("qo")
    t_qvall = NOW("t")
    v_qvall = NOW("v")
    E("vector", None, waits=[("qo", NOW("qo"))], inc="v")
    _ckpt("qv", [(ar2(k), 128 * k) for k in range(KT)])
    _ckpt("qvv", [(C_svec[:], 0), (C_cvec[:], 128)])

    tg_hist = []
    for k in range(KT):
        sb = (k % 2) * T
        wl = [("t", NOW("t")), ("qo", NOW("qo"))] if k < 2 else [("t", tg_hist[k - 2])]
        E("sync", (lambda kk, s_: (lambda eng: eng.dma_start(
            out=STRM[:, s_:s_ + T], in_=xprevT[kk * P:(kk + 1) * P, :])))(k, sb),
          waits=wl, inc="qi", inc_by=16)
        dd = NOW("qi")

        def mk_psum(kk, s_):
            def body(eng):
                inst = None
                for c in range(TC):
                    inst = eng.matmul(pG[64:65, c * 512:(c + 1) * 512], C_misc[:, 8:9],
                                      STRM[:, s_ + c * 512:s_ + (c + 1) * 512],
                                      start=(kk == 0), stop=(kk == KT - 1))
                return inst
            return body
        E("tensor", mk_psum(k, sb), waits=[("qi", dd)], inc="t")
        t_raw = NOW("t")
        wl_sq = [("qi", dd), ("s", NOW("s"))]
        if k == 0:
            wl_sq.append(("qo", NOW("qo")))
        E("scalar", (lambda kk, s_: (lambda eng: eng.activation(
            SCR[:, s_:s_ + T], STRM[:, s_:s_ + T], AF.Square)))(k, sb),
          waits=wl_sq, inc="s")
        s_h = NOW("s")

        SQACC = [(pA, 0), (pA, 32), (pA, 64), (pB, 0)]

        def mk_psumsq(kk, s_):
            def body(eng):
                inst = None
                for c in range(TC):
                    pp, rb = SQACC[c]
                    inst = eng.matmul(pp[rb:rb + 1, :], C_misc[:, 8:9],
                                      SCR[:, s_ + c * 512:s_ + (c + 1) * 512],
                                      start=(kk == 0), stop=(kk == KT - 1))
                return inst
            return body
        wl_sq2 = [("s", s_h)]
        if k == 0:
            wl_sq2.append(("v", v_qvall))
        E("tensor", mk_psumsq(k, sb), waits=wl_sq2, inc="t")
        E("scalar", (lambda kk, s_: (lambda eng: eng.activation(
            STRM[:, s_:s_ + T], STRM[:, s_:s_ + T], AF.Copy,
            scale=C_der[:, kk:kk + 1])))(k, sb),
          waits=[("t", t_raw)], inc="s")
        s_sc = NOW("s")

        def mk_g(kk, s_):
            def body(eng):
                inst = None
                for c in range(TC):
                    inst = eng.matmul(pG[0:NF, c * 512:(c + 1) * 512],
                                      C_wqf[:, kk * NF:(kk + 1) * NF],
                                      STRM[:, s_ + c * 512:s_ + (c + 1) * 512],
                                      start=(kk == 0), stop=(kk == KT - 1))
                return inst
            return body
        E("tensor", mk_g(k, sb), waits=[("s", s_sc)], inc="t")
        tg_hist.append(NOW("t"))
    t_g = NOW("t")

    def mk_psrow(eng):
        eng.tensor_copy(SROW0, pG[64:65, :])
        for c in range(TC):
            pp, rb = [(pA, 0), (pA, 32), (pA, 64), (pB, 0)][c]
            eng.tensor_copy(SROW1[:, c * 512:(c + 1) * 512], pp[rb:rb + 1, :])
        return eng.tensor_copy(ar1(7, NF), pG[0:NF, :])
    E("vector", mk_psrow, waits=[("t", t_g)], inc="v")
    v_psr = NOW("v")
    v_tm2 = stats_to_tm(v_psr, None)
    v_ln2 = ln_smallmath(v_tm2)
    v_rp = replicate16(32, v_ln2, RREP)
    v_mup = replicate16(48, v_rp, MUREP)

    b0_hist = []
    for c in range(TC):
        ps = pAB[c % 2]

        def mk_b0(cc, ps_):
            def body(eng):
                inst = None
                for k in range(KT):
                    inst = eng.matmul(ps_[0:NF, :], C_wfs[:, k * P + NF:(k + 1) * P],
                                      ar2(k)[:, cc * 512:(cc + 1) * 512],
                                      start=(k == 0), stop=(k == KT - 1))
                return inst
            return body
        wl_b0 = [("v", v_psr)]
        if c >= 2:
            wl_b0.append(("v", b0_hist[c - 2]))
        E("tensor", mk_b0(c, ps), waits=wl_b0, inc="t")
        tb = NOW("t")
        E("vector", (lambda cc, ps_: (lambda eng: eng.tensor_copy(
            ar1(1, NF)[:, cc * 512:(cc + 1) * 512], ps_[0:NF, :])))(c, ps),
          waits=[("t", tb)], inc="v")
        b0_hist.append(NOW("v"))

    def mk_aaux(eng):
        As = ar1(0, NF)
        Gs = ar1(7, NF)
        t_ = ar1(4, NF)
        eng.tensor_tensor(t_, C_nf[:, 1:2].broadcast_to([NF, T]), MUREP[0:NF, :], OP.mult)
        eng.tensor_tensor(As, Gs, RREP[0:NF, :], OP.mult)
        eng.tensor_tensor(As, As, t_, OP.add)
        eng.tensor_scalar(MTM[0:NF, 0:1], C_nf[:, 4:5], -1.0, None, OP.mult)
        eng.drain()
        eng.tensor_scalar(MTM[0:NF, 0:1], MTM[0:NF, 0:1], 1.0, None, OP.add)
        eng.drain()
        eng.tensor_tensor(As[:, 0:1], As[:, 0:1], MTM[0:NF, 0:1], OP.mult)
        eng.drain()
        return eng.tensor_tensor(As[:, 0:1], As[:, 0:1], C_nf[:, 5:6], OP.add)
    E("vector", mk_aaux, waits=[("v", v_mup)], inc="v")
    v_b0 = NOW("v")
    _ckpt("ab0", [(ar1(0), 0), (ar1(1), 128), (RREP[:], 256), (MUREP[:], 384),
                  (ar1(7), 512)])

    def f_assemble(nl, bias_sin, bias_cos, wait_v):
        As, Bs, F2 = ar1(0, NF), ar1(1, NF), ar1(2)

        def body(eng):
            if nl > 0:
                eng.tensor_tensor(F2[0:NF, nl:T], As[:, 0:T - nl], Bs[:, nl:T], OP.add)
                return eng.tensor_tensor(F2[0:NF, 0:nl],
                                         C_nf[:, 0:1].broadcast_to([NF, nl]),
                                         Bs[:, 0:nl], OP.add)
            return eng.tensor_tensor(F2[0:NF, :], As[:, :], Bs[:, :], OP.add)
        E("vector", body, waits=[("v", wait_v)], inc="v")
        vf = NOW("v")

        def pack(eng):
            eng.activation(F2[NF:P, :], F2[0:NF, :], AF.Identity, bias=bias_cos)
            return eng.activation(F2[0:NF, :], F2[0:NF, :], AF.Identity, bias=bias_sin)
        E("scalar", pack, waits=[("v", vf), ("s", NOW("s"))], inc="s")
        return NOW("s")

    def sin_stage(wait_s):
        F2, Sb, s1_ = ar1(2), ar1(3), ar1(4)
        E("scalar", lambda eng: eng.activation(s1_, F2[:], AF.Sign,
                                               bias=C_misc[:, 11:12]),
          waits=[("s", wait_s), ("v", NOW("v"))], inc="s")
        E("scalar", lambda eng: eng.activation(ar1(5), F2[:], AF.Sign,
                                               bias=C_misc[:, 12:13]), inc="s")
        ss = NOW("s")

        def red(eng):
            eng.tensor_tensor(s1_, s1_, ar1(5), OP.add)
            eng.scalar_tensor_tensor(F2[:], s1_, -C1H, F2[:], OP.mult, OP.add)
            return eng.scalar_tensor_tensor(F2[:], s1_, -C2H, F2[:], OP.mult, OP.add)
        E("vector", red, waits=[("s", ss)], inc="v")
        vr = NOW("v")
        E("scalar", lambda eng: eng.activation(Sb[:], F2[:], AF.Sin),
          waits=[("v", vr)], inc="s")
        return NOW("s")

    def z_stage(wait_s, guard_v):
        Sb = ar1(3)
        zev = []
        for mz in range(KT):
            for c in range(TC):
                j = mz * TC + c
                ps = pAB[j % 2]

                def mk_z(m2, c2, ps_):
                    def body(eng):
                        return eng.matmul(ps_[:], C_outw[:, m2 * P:(m2 + 1) * P],
                                          Sb[:, c2 * 512:(c2 + 1) * 512],
                                          start=True, stop=True)
                    return body
                wl = [("s", wait_s if j < 2 else zev[j - 2])]
                E("tensor", mk_z(mz, c, ps), waits=wl, inc="t")
                tz = NOW("t")
                wl_e = [("t", tz)]
                if j == 0:
                    wl_e.append(("v", guard_v))
                E("scalar", (lambda m2, c2, ps_: (lambda eng: eng.activation(
                    ar2(m2)[:, c2 * 512:(c2 + 1) * 512], ps_[:], AF.Identity,
                    bias=C_outb[:, m2:m2 + 1])))(mz, c, ps),
                  waits=wl_e, inc="s")
                zev.append(NOW("s"))
        E("scalar", lambda eng: eng.drain(), inc="s")
        return NOW("s")

    def mmnorm_stage(wait_s):
        acc = ar1(6)
        zview = AR2[:].rearrange("p (d t) -> p t d", d=KT)
        tgs = []
        for h in range(2):
            E("vector", (lambda h_: (lambda eng: eng.tensor_reduce(
                acc[:, h_ * 1024:(h_ + 1) * 1024].unsqueeze(2),
                zview[:, h_ * 1024:(h_ + 1) * 1024, :], AX.X, OP.max,
                apply_absolute_value=True)))(h),
              waits=[("s", wait_s)], inc="v")
            v0 = NOW("v")
            if h == 1:
                mmnorm_stage.v_zread = v0
            tgs.append(tr_batch([acc[:, (8 * h + c) * P:(8 * h + c + 1) * P]
                                 for c in range(8)], [("v", v0)], off=8 * h))

        def reds(h_):
            def body(eng):
                inst = None
                for c2 in range(8 * h_, 8 * h_ + 8):
                    inst = eng.tensor_reduce(MTM[:, 32 + c2:33 + c2],
                                             pG[:, c2 * P:(c2 + 1) * P], AX.X, OP.max)
                return inst
            return body
        E("vector", reds(0), waits=[("t", tgs[0])], inc="v")
        E("vector", reds(1), waits=[("t", tgs[1])], inc="v")
        vtm = NOW("v")

        def rmath(eng):
            eng.drain()
            eng.tensor_scalar(MTM[:, 32:48], MTM[:, 32:48], EPS_MM, None, OP.add)
            eng.drain()
            return eng.reciprocal(MTM[:, 32:48], MTM[:, 32:48])
        E("vector", rmath, waits=[("v", vtm)], inc="v")
        return replicate16(32, NOW("v"), RREP)

    def ab_stage(wait_v, wait_s_z):
        evs = []
        for c in range(TC):
            ps = pAB[c % 2]

            def mk_ab(cc, ps_):
                def body(eng):
                    inst = None
                    for k in range(KT):
                        inst = eng.matmul(ps_[:], C_wfs[:, k * P:(k + 1) * P],
                                          ar2(k)[:, cc * 512:(cc + 1) * 512],
                                          start=(k == 0), stop=(k == KT - 1))
                    return inst
                return body
            wl = [("s", wait_s_z)]
            if c == 0 and swp.get("vflush") is not None:
                wl.append(("v", swp["vflush"]))
            if c >= 2:
                wl.append(("v", evs[c - 2]))
            E("tensor", mk_ab(c, ps), waits=wl, inc="t")
            tab = NOW("t")

            def mk_ev(cc, ps_):
                def body(eng):
                    cs = slice(cc * 512, (cc + 1) * 512)
                    eng.tensor_tensor(ar1(0, NF)[:, cs], RREP[0:NF, cs],
                                      ps_[0:NF, :], OP.mult)
                    return eng.tensor_tensor(ar1(1, NF)[:, cs], RREP[0:NF, cs],
                                             ps_[NF:P, :], OP.mult)
                return body
            E("vector", mk_ev(cc=c, ps_=ps), waits=[("t", tab), ("v", wait_v)], inc="v")
            evs.append(NOW("v"))
        return NOW("v")

    WP_SCAN = 16
    swp = {"qi": {}, "t": {}, "st": {"wg": {}, "wu": {}}}
    swp_regs = (SCR[:, 0:512], SCR[:, 512:1024], SCR[:, 1024:1536],
                SCR[:, 1536:2048], SCR[:, 5120:5632], SCR[:, 5632:6144])

    def swp_fam(j):
        return (wg_nat, wgspill, "wg") if j % 2 == 0 else (wu_nat, wuspill, "wu")

    def swp_load(j):
        nat, _, _ = swp_fam(j)
        ch = j // 2
        wl = [("t", swp["t"][j - 2])] if j >= 2 else \
            [("v", NOW("v")), ("t", NOW("t"))]
        E("sync", (lambda n_, c2, b_: (lambda eng: eng.dma_start(
            out=STRM[:, b_ * C:(b_ + 1) * C], in_=n_[c2 * P:(c2 + 1) * P, :])))(
                nat, ch, j % 2),
          waits=wl, inc="qi", inc_by=16)
        swp["qi"][j] = NOW("qi")

    def tr_batch_ab(srcs, wait_list):
        def body(eng):
            inst = None
            for i, s_ in enumerate(srcs):
                pp = pA if i < 4 else pB
                ii = i % 4
                inst = eng.matmul(pp[:, ii * P:(ii + 1) * P], s_, C_eye[:],
                                  is_transpose=True, start=(ii == 0), stop=(ii == 3))
            return inst
        E("tensor", body, waits=wait_list, inc="t")
        E("tensor", lambda eng: eng.matmul(pT0[0:1, 0:1], C_eye[:, 0:1],
                                           C_eye[:, 0:1], start=True, stop=True),
          inc="t")
        return NOW("t")

    def swp_flush():
        if swp.get("pend") is None:
            return
        tg, spill, ch = swp.pop("pend")

        def cp(eng):
            eng.tensor_copy(SCR[:, 4096:4096 + 512], pA[:])
            return eng.tensor_copy(SCR[:, 4608:4608 + 512], pB[:])
        E("vector", cp, waits=[("t", tg)], inc="v")
        swp["vflush"] = NOW("v")
        v_st = NOW("v")
        E("sync", (lambda sp, c2: (lambda eng: eng.dma_start(
            out=sp[c2, 0:KT].rearrange("d p j -> p d j"),
            in_=SCR[:, 4096:4096 + KT * P].rearrange("p (d j) -> p d j", j=P))))(
                spill, ch),
          waits=[("v", v_st)], inc="qi", inc_by=16)
        E("vector", None, waits=[("qi", NOW("qi"))], inc="v")

    def swp_slot(j, s_z_cur):
        if j >= 2 * WP_SCAN:
            return
        swp_flush()
        if j not in swp["qi"]:
            swp_load(j)
        if j + 1 < 2 * WP_SCAN and (j + 1) not in swp["qi"]:
            swp_load(j + 1)
        nat, spill, fam = swp_fam(j)
        ch = j // 2
        src_ = STRM[:, (j % 2) * C:(j % 2) * C + C]
        wl_q = [("qi", swp["qi"][j]), ("v", NOW("v"))]
        if ch == 0:
            wl_q.append(("g", NOW("g")))
        qw = quant_emit(src_, C, swp_regs, MUREP[:, 0:128], wl_q,
                        swp["st"][fam], sign_dve=True)
        tg = tr_batch_ab([src_[:, i * P:(i + 1) * P] for i in range(KT)],
                         [qw, ("v", NOW("v")), ("s", s_z_cur)])
        swp["pend"] = (tg, spill, ch)
        swp["t"][j] = NOW("t")

    s_pack = f_assemble(0, C_nf[:, 6:7], C_nf[:, 7:8], v_b0)
    s_sin = sin_stage(s_pack)
    _ckpt("f1", [(ar1(2), 0), (ar1(3), 128)])
    s_z = z_stage(s_sin, NOW("v"))
    v_r = mmnorm_stage(s_z)
    v_zread = mmnorm_stage.v_zread
    _ckpt("scan1", [(ar2(k), 128 * k) for k in range(KT)])
    _ckpt("scan1r", [(RREP[:], 0)])
    it_no = 1
    for nl in NLS:
        it_no += 1
        v_ab = ab_stage(v_r, s_z)
        s_pack = f_assemble(nl, C_nf[:, 3:4], C_misc[0:NF, 10:11], v_ab)
        s_sin = sin_stage(s_pack)
        s_z = z_stage(s_sin, v_zread)
        swp_slot(3 * (it_no - 2), s_z)
        swp_slot(3 * (it_no - 2) + 1, s_z)
        swp_slot(3 * (it_no - 2) + 2, s_z)
        v_r = mmnorm_stage(s_z)
        v_zread = mmnorm_stage.v_zread
        swp_flush()
        _ckpt("scan%d" % it_no, [(ar2(k), 128 * k) for k in range(KT)])

    y_hist = []
    for k in range(KT):
        ss = (k % 2) * T
        wl = [("v", v_r)] if k < 2 else [("v", y_hist[k - 2])]
        E("sync", (lambda kk, s_: (lambda eng: eng.dma_start(
            out=SCR[:, s_:s_ + T], in_=vspill[kk * P:(kk + 1) * P, :])))(k, ss),
          waits=wl, inc="qi", inc_by=16)
        dd = NOW("qi")

        def mk_y(kk, s_):
            def body(eng):
                eng.tensor_tensor(ar2(kk), ar2(kk), RREP[:], OP.mult)
                return eng.tensor_tensor(ar2(kk), ar2(kk), SCR[:, s_:s_ + T], OP.mult)
            return body
        E("vector", mk_y(k, ss), waits=[("qi", dd)], inc="v")
        y_hist.append(NOW("v"))
    v_y = NOW("v")
    _ckpt("ydump", [(ar2(k), 128 * k) for k in range(KT)])

    tp_hist = []
    pe_hist = []
    for m in range(KT):
        sl = (m % 2) * C
        wl = [("v", v_y)] if m < 2 else [("t", tp_hist[m - 2])]
        E("sync", (lambda mm, s_: (lambda eng: eng.dma_start(
            out=STRM[:, s_:s_ + C], in_=proj_mk[:, mm * C:(mm + 1) * C])))(m, sl),
          waits=wl, inc="qi", inc_by=16)
        ss = (m % 2) * T
        wl2 = [("v", v_y)] if m < 2 else [("v", pe_hist[4 * (m - 2) + 3])]
        E("sync", (lambda mm, s_: (lambda eng: eng.dma_start(
            out=SCR[:, s_:s_ + T], in_=xT[mm * P:(mm + 1) * P, :])))(m, ss),
          waits=wl2, inc="qi", inc_by=16)
        dd = NOW("qi")
        E("scalar", (lambda mm, s_: (lambda eng: (eng.activation(
            SCR[:, s_:s_ + T], SCR[:, s_:s_ + T], AF.Identity,
            bias=C_misc[:, mm:mm + 1]), eng.drain())[1]))(m, ss),
          waits=[("qi", dd), ("s", NOW("s"))], inc="s")
        s_xp = NOW("s")
        for c in range(TC):
            ps = pAB[c % 2]
            j = m * TC + c

            def mk_proj(s_, cc, ps_):
                def body(eng):
                    inst = None
                    for k in range(KT):
                        inst = eng.matmul(ps_[:], STRM[:, s_ + k * P:s_ + (k + 1) * P],
                                          ar2(k)[:, cc * 512:(cc + 1) * 512],
                                          start=(k == 0), stop=(k == KT - 1))
                    return inst
                return body
            wl_mm = [("qi", dd)]
            wl_mm.append(("v", pe_hist[j - 2]) if j >= 2 else ("v", v_y))
            E("tensor", mk_proj(sl, c, ps), waits=wl_mm, inc="t")
            tp = NOW("t")
            wl_e = [("t", tp)] if c > 0 else [("t", tp), ("s", s_xp)]
            E("vector", (lambda mm, s2, cc, ps_: (lambda eng: eng.scalar_tensor_tensor(
                ar1(mm)[:, cc * 512:(cc + 1) * 512], ps_[:], C_der[:, 16 + mm:17 + mm],
                SCR[:, s2 + cc * 512:s2 + (cc + 1) * 512], OP.mult, OP.add)))(m, ss, c, ps),
              waits=wl_e, inc="v")
            pe_hist.append(NOW("v"))
        tp_hist.append(NOW("t"))
    v_x2 = NOW("v")

    _ckpt("x2", [(ar1(k), 128 * k) for k in range(KT)])
    for k in range(KT):
        E("gpsimd", (lambda kk: (lambda eng: eng.dma_start(
            out=x2spill[kk * P:(kk + 1) * P, :], in_=ar1(kk))))(k),
          waits=[("v", v_x2)], inc="qo", inc_by=16)
    qo_x2 = NOW("qo")

    tsq_hist = []
    for k in range(KT):
        sb2 = (k % 2) * T

        def mk_sum2(kk):
            def body(eng):
                inst = None
                for c in range(TC):
                    inst = eng.matmul(pG[0:1, c * 512:(c + 1) * 512], C_misc[:, 8:9],
                                      ar1(kk)[:, c * 512:(c + 1) * 512],
                                      start=(kk == 0), stop=(kk == KT - 1))
                return inst
            return body
        E("tensor", mk_sum2(k), waits=[("v", v_x2)], inc="t")
        wl_sq = [("v", v_x2), ("s", NOW("s"))]
        if k >= 2:
            wl_sq.append(("t", tsq_hist[k - 2]))
        E("scalar", (lambda kk, s_: (lambda eng: eng.activation(
            SCR[:, s_:s_ + T], ar1(kk), AF.Square)))(k, sb2),
          waits=wl_sq, inc="s")
        s_h = NOW("s")

        def mk_sumsq2(kk, s_):
            def body(eng):
                inst = None
                for c in range(TC):
                    inst = eng.matmul(pG[32:33, c * 512:(c + 1) * 512], C_misc[:, 8:9],
                                      SCR[:, s_ + c * 512:s_ + (c + 1) * 512],
                                      start=(kk == 0), stop=(kk == KT - 1))
                return inst
            return body
        E("tensor", mk_sumsq2(k, sb2), waits=[("s", s_h)], inc="t")
        tsq_hist.append(NOW("t"))
    t_st2 = NOW("t")

    def mk_srow2(eng):
        eng.tensor_copy(SROW0, pG[0:1, :])
        return eng.tensor_copy(SROW1, pG[32:33, :])
    E("vector", mk_srow2, waits=[("t", t_st2)], inc="v")
    v_tm3 = stats_to_tm(NOW("v"), None)
    v_ln3 = ln_smallmath(v_tm3)

    def mk_rowstack(eng):
        eng.tensor_copy(MTM[:, 0:8], C_der[:, 8:16])
        return eng.tensor_copy(MTM[:, 8:16], C_modv[:, 24:32])
    E("vector", mk_rowstack, waits=[("v", v_ln3)], inc="v")
    v_rs = NOW("v")
    E("tensor", lambda eng: eng.matmul(pT0[0:16, :], MTM[:, 0:16], C_eye[:],
                                       is_transpose=True, start=True, stop=True),
      waits=[("v", v_rs)], inc="t")
    E("tensor", lambda eng: eng.matmul(pG[0:1, 0:1], C_eye[:, 0:1], C_eye[:, 0:1],
                                       start=True, stop=True), inc="t")
    t_rs = NOW("t")

    def mk_rowrep(eng):
        eng.tensor_copy(MTT[:], pT0[0:16, :])
        eng.drain()
        bc = MTT[:].unsqueeze(1).broadcast_to([16, KT, P])
        eng.tensor_tensor(RHSM[:, 0:C].rearrange("k (c j) -> k c j", j=P),
                          C_selrowa[:].rearrange("k (c j) -> k c j", j=P), bc, OP.mult)
        return eng.tensor_tensor(RHSM[:, C:2 * C].rearrange("k (c j) -> k c j", j=P),
                                 C_selrowb[:].rearrange("k (c j) -> k c j", j=P), bc, OP.mult)
    E("vector", mk_rowrep, waits=[("t", t_rs)], inc="v")
    v_rr = NOW("v")

    def mk_rowrep_mm(eng):
        inst = None
        for c in range(4):
            inst = eng.matmul(pG[:, c * 512:(c + 1) * 512], C_ones16[:],
                              RHSM[:, c * 512:(c + 1) * 512], start=True, stop=True)
        return inst
    E("tensor", mk_rowrep_mm, waits=[("v", v_rr)], inc="t")
    t_rr = NOW("t")
    E("scalar", lambda eng: (eng.activation(MUREP[:, 0:2 * C], pG[:, 0:2 * C], AF.Copy),
                             eng.drain())[1],
      waits=[("t", t_rr)], inc="s")
    s_rowrep = NOW("s")
    E("vector", None, waits=[("s", s_rowrep)], inc="v")
    _ckpt("ln2", [(MUREP[:], 0), (MTM[:], 128), (MTT[:], 256), (RHSM, 384)])

    wp_state = {"wgu": {}, "wd": {}}

    def wprep(nat, spill, nchunks, width, use_scr, start=0):
        ntile = width // P
        if use_scr:
            regs = (STRM[:, 0:512], STRM[:, 512:1024], STRM[:, 1024:1536],
                    STRM[:, 1536:2048], SCR[:, 4256:4768], SCR[:, 4768:5280])
            smalls = SCR[:, 2816:3232]
            stage0 = 3232
            state = wp_state["wd"]
        else:
            regs = (SCR[:, 0:512], SCR[:, 512:1024], SCR[:, 1024:1536],
                    SCR[:, 1536:2048], SCR[:, 3584:4096], SCR[:, 4096:4608])
            smalls = SCR[:, 2048:2432]
            stage0 = 2560
            state = wp_state["wgu"]

        def srcbuf(c2):
            if use_scr:
                return SCR[:, 0:width]
            return STRM[:, (c2 % 2) * C:(c2 % 2) * C + width]

        qi_load = {}
        t_trs = {}

        def emit_load(c2):
            if use_scr:
                wl = [("v", NOW("v")), ("s", s_rowrep), ("t", NOW("t"))]
            elif c2 < start + 2:
                wl = [("s", s_rowrep), ("v", NOW("v")), ("t", NOW("t"))]
            else:
                wl = [("t", t_trs[c2 - 2])]
            E("sync", (lambda n_, c3, sr: (lambda eng: eng.dma_start(
                out=sr, in_=n_[c3 * P:(c3 + 1) * P, :])))(nat, c2, srcbuf(c2)),
              waits=wl, inc="qi", inc_by=16)
            qi_load[c2] = NOW("qi")

        emit_load(start)
        for ch in range(start, nchunks):
            src_ = srcbuf(ch)
            if ch + 1 < nchunks and not use_scr:
                emit_load(ch + 1)
            wl_q = [("qi", qi_load[ch]), ("v", NOW("v"))]
            if ch == start:
                wl_q.append(("g", NOW("g")))
            qsem, qval = quant_emit(src_, width, regs, smalls, wl_q, state)
            g0 = 0
            while g0 < ntile:
                gn = min(8, ntile - g0)
                wl3 = [(qsem, qval), ("v", NOW("v"))]
                tg = tr_batch([src_[:, (g0 + i) * P:(g0 + i + 1) * P]
                               for i in range(gn)], wl3)
                E("vector", (lambda gn_, s0: (lambda eng: eng.tensor_copy(
                    SCR[:, s0:s0 + gn_ * P], pG[:, 0:gn_ * P])))(gn, stage0),
                  waits=[("t", tg)], inc="v")
                v_st = NOW("v")
                E("sync", (lambda c2, g_, gn_, s0: (lambda eng: eng.dma_start(
                    out=spill[c2, g_:g_ + gn_].rearrange("d p j -> p d j"),
                    in_=SCR[:, s0:s0 + gn_ * P].rearrange("p (d j) -> p d j", j=P))))(
                        ch, g0, gn, stage0),
                  waits=[("v", v_st)], inc="qi", inc_by=16)
                E("vector", None, waits=[("qi", NOW("qi"))], inc="v")
                g0 += gn
            t_trs[ch] = NOW("t")
            if use_scr and ch + 1 < nchunks:
                emit_load(ch + 1)

    wprep(wg_nat, wgspill, IKT, C, use_scr=False, start=WP_SCAN)
    wprep(wu_nat, wuspill, IKT, C, use_scr=False, start=WP_SCAN)
    wprep(wd_nat, wdspill, KT, I, use_scr=True)

    s_evh = None
    v_cout = None
    h1_state = {}
    h1_g = []
    for tt in range(NTT):
        SX = SCR[:, (tt % 2) * 1024:(tt % 2) * 1024 + 1024]
        wl1 = [("s", s_evh)] if tt >= 1 else [("v", NOW("v")), ("s", s_rowrep)]
        tg = tr_batch([ar1(k)[:, tt * P:(tt + 1) * P] for k in range(KT)], wl1)
        wl_ev = [("t", tg)]
        if tt >= 2:
            wl_ev.append(h1_g[tt - 2])
        E("scalar", (lambda t2, SX_: (lambda eng: (eng.activation(
            SX_, pG[:, 0:C], AF.Identity, scale=MTM[:, 32 + t2:33 + t2],
            bias=MTM[:, 48 + t2:49 + t2]), eng.drain())[1]))(tt, SX),
          waits=wl_ev, inc="s")
        s_evh = NOW("s")

        def mk_ln2(SX_):
            def body(eng):
                eng.tensor_tensor(SX_, SX_, MUREP[:, 0:C], OP.mult)
                return eng.tensor_tensor(SX_, SX_, MUREP[:, C:2 * C], OP.add)
            return body
        E("vector", mk_ln2(SX), waits=[("s", s_evh)], inc="v")
        h1_regs = (SCR[:, 2048:2560], SCR[:, 2560:3072], SCR[:, 3072:3584],
                   SCR[:, 3584:4096], SCR[:, 4480:4992], SCR[:, 4992:5504])
        wl_q = [("v", NOW("v"))]
        if tt == 0:
            wl_q.append(("g", NOW("g")))
        qsem, qval = quant_emit(SX, C, h1_regs, SCR[:, 4096:4480], wl_q, h1_state)
        h1_g.append((qsem, qval))
        wl2 = [(qsem, qval)] if tt == 0 else [(qsem, qval), ("v", v_cout)]
        tg2 = tr_batch([SX[:, k * P:(k + 1) * P] for k in range(KT)], wl2, off=KT)
        E("vector", (lambda t2: (lambda eng: eng.tensor_copy(
            AR2[:].rearrange("p (d t) -> p d t", t=T)[:, :, t2 * P:(t2 + 1) * P],
            pG[:, C:2 * C].rearrange("p (d j) -> p d j", j=P))))(tt),
          waits=[("t", tg2)], inc="v")
        v_cout = NOW("v")
    v_h1q = v_cout
    _ckpt("h1q", [(ar2(k), 128 * k) for k in range(KT)])

    CWS = [512, 512, 512, 512, 512, 256]
    tcch_hist = []
    steps = [(cch, Tc) for cch in range(6) for Tc in range(TC)]
    pend_q = None

    def s3a_tail(p):
        cch_, Tc_, cw_, ctn_, hb_, qw_ = p
        srcs = [SCR[:, hb_ + mt2 * cw_ + ct2 * P:hb_ + mt2 * cw_ + (ct2 + 1) * P]
                for mt2 in range(4) for ct2 in range(ctn_)]
        wl_tr = [qw_]
        if s3a_tail.v_prev is not None:
            wl_tr.append(("v", s3a_tail.v_prev))
        tg2 = tr_batch(srcs, wl_tr)
        E("vector", (lambda c_: (lambda eng: eng.tensor_copy(
            RREP[:, 0:c_ * 512].rearrange("p (ct mt j) -> p mt ct j", ct=c_, mt=4),
            pG[:, 0:4 * c_ * P].rearrange("p (mt ct j) -> p mt ct j", mt=4, ct=c_))))(ctn_),
          waits=[("t", tg2)], inc="v")
        v_stage = NOW("v")
        E("sync", (lambda c2, T2, c_: (lambda eng: eng.dma_start(
            out=hqspill[c2, T2][:, 0:c_ * 512], in_=RREP[:, 0:c_ * 512])))(cch_, Tc_, ctn_),
          waits=[("v", v_stage)], inc="qi", inc_by=16)
        E("vector", None, waits=[("qi", NOW("qi"))], inc="v")
        s3a_tail.v_prev = NOW("v")

    s3a_tail.v_prev = None
    s3a_vhm = [None]
    s3a_state = {}
    dd_map = {}

    def s3a_wloads(c3):
        cw3 = CWS[c3]
        for (spill, base) in [(wgspill, (c3 % 2) * 4096),
                              (wuspill, 8192 + (c3 % 2) * 4096)]:
            for d in range(KT):
                if c3 < 2:
                    wl = [("v", v_h1q), ("qo", qo_x2)]
                else:
                    wl = [("t", tcch_hist[c3 - 2])]
                E("gpsimd", (lambda sp, d2, c2, b_, cw2: (lambda eng: eng.dma_start(
                    out=AR1[:, b_ + d2 * 512:b_ + d2 * 512 + cw2]
                    .rearrange("p (c j) -> p c j", j=P),
                    in_=sp[4 * c2:4 * c2 + cw2 // P, d2]
                    .rearrange("c p j -> p c j"))))(spill, d, c3, base, cw3),
                  waits=wl, inc="qo", inc_by=16)
        dd_map[c3] = NOW("qo")

    s3a_wloads(0)
    for si, (cch, Tc) in enumerate(steps):
        cw = CWS[cch]
        ctiles = cw // P
        wgb = (cch % 2) * 4096
        wub = 8192 + (cch % 2) * 4096
        dd = dd_map[cch]
        if Tc == 1 and cch + 1 < 6:
            s3a_wloads(cch + 1)
        if Tc == 0:
            if cch == 0:
                for kt in range(16, 20):
                    E("sync", (lambda k2: (lambda eng: eng.dma_start(
                        out=STRM[:, (k2 - 16) * C:(k2 - 15) * C]
                        .rearrange("p (d j) -> p d j", j=P),
                        in_=wdspill[:, k2].rearrange("d p j -> p d j"))))(kt),
                      waits=[("v", NOW("v")), ("t", NOW("t"))], inc="qi", inc_by=16)
        hb = (si % 2) * 2048
        for mt in range(4):
            tok0 = Tc * 512 + mt * P

            def mk_gate(t0, cw2, b_):
                def body(eng):
                    inst = None
                    for k in range(KT):
                        inst = eng.matmul(pA[:, 0:cw2], ar2(k)[:, t0:t0 + P],
                                          AR1[:, b_ + k * 512:b_ + k * 512 + cw2],
                                          start=(k == 0), stop=(k == KT - 1))
                    return inst
                return body
            E("tensor", mk_gate(tok0, cw, wgb),
              waits=[("qo", dd), ("s", NOW("s"))], inc="t")
            tgate = NOW("t")
            wl_sil = [("t", tgate)]
            if s3a_vhm[0] is not None:
                wl_sil.append(("v", s3a_vhm[0]))
            E("scalar", (lambda cw2: (lambda eng: (eng.activation(
                MUREP[:, 0:cw2], pA[:, 0:cw2], AF.Silu),
                eng.drain())[1]))(cw),
              waits=wl_sil, inc="s")

            def mk_up(t0, cw2, b_):
                def body(eng):
                    inst = None
                    for k in range(KT):
                        inst = eng.matmul(pB[:, 0:cw2], ar2(k)[:, t0:t0 + P],
                                          AR1[:, b_ + k * 512:b_ + k * 512 + cw2],
                                          start=(k == 0), stop=(k == KT - 1))
                    return inst
                return body
            wl_up = [] if s3a_vhm[0] is None else [("v", s3a_vhm[0])]
            E("tensor", mk_up(tok0, cw, wub), waits=wl_up, inc="t")
            tup = NOW("t")
            E("vector", (lambda cw2, mt2, h_: (lambda eng: eng.tensor_tensor(
                SCR[:, h_ + mt2 * cw2:h_ + (mt2 + 1) * cw2], MUREP[:, 0:cw2],
                pB[:, 0:cw2], OP.mult)))(cw, mt, hb),
              waits=[("t", tup), ("s", NOW("s"))], inc="v")
            s3a_vhm[0] = NOW("v")
            if pend_q is not None and pend_q["done"] < pend_q["n"]:
                pend_q["ret"] = pend_q["emit"](pend_q["done"])
                pend_q["done"] += 1
        if Tc == TC - 1:
            tcch_hist.append(NOW("t"))
        vh = NOW("v")
        if pend_q is not None:
            while pend_q["done"] < pend_q["n"]:
                pend_q["ret"] = pend_q["emit"](pend_q["done"])
                pend_q["done"] += 1
            s3a_tail(pend_q["meta"] + (pend_q["ret"],))
        s3a_regs = (SCR[:, 4096:4608], SCR[:, 4608:5120], SCR[:, 5120:5632],
                    SCR[:, 5632:6144], MUREP[:, 512:1024], MUREP[:, 1024:1536])
        wl_q = [("v", vh)]
        if si == 0:
            wl_q.append(("g", NOW("g")))
        nq, eq = quant_chunks(SCR[:, hb:hb + 4 * cw], 4 * cw, s3a_regs,
                              MUREP[:, 1536:1920], wl_q, s3a_state, sign_dve=True)
        pend_q = {"emit": eq, "n": nq, "done": 0, "ret": None,
                  "meta": (cch, Tc, cw, ctiles, hb)}
    while pend_q["done"] < pend_q["n"]:
        pend_q["ret"] = pend_q["emit"](pend_q["done"])
        pend_q["done"] += 1
    s3a_tail(pend_q["meta"] + (pend_q["ret"],))
    qi_hq = NOW("qi")
    t_s3a = NOW("t")

    def wd_slice_ap(kt):
        if kt < 16:
            return AR1[:, kt * C:(kt + 1) * C]
        if kt < 20:
            return STRM[:, (kt - 16) * C:(kt - 15) * C]
        return SCR[:, 4096 + (kt - 20) * C:4096 + (kt - 19) * C]

    for kt in list(range(16)) + [20, 21]:
        wl = [("t", t_s3a), ("v", NOW("v"))]
        E("sync", (lambda k2: (lambda eng: eng.dma_start(
            out=wd_slice_ap(k2).rearrange("p (d j) -> p d j", j=P),
            in_=wdspill[:, k2].rearrange("d p j -> p d j"))))(kt),
          waits=wl, inc="qi", inc_by=16)
    qi_wd = NOW("qi")

    def x2slice(T2, m2):
        if T2 % 2 == 0:
            return SCR[:, m2 * 512:(m2 + 1) * 512]
        if m2 < 4:
            return RREP[:, m2 * 512:(m2 + 1) * 512]
        return MUREP[:, (m2 - 4) * 512:(m2 - 3) * 512]

    res_hist = []
    ttc_hist = []
    for Tc in range(TC):
        for kt in range(IKT):
            cc2, ct2 = kt // 4, kt % 4
            wl = [("qi", qi_hq), ("v", v_h1q)]
            if Tc > 0:
                wl.append(("t", ttc_hist[Tc - 1]))
            qeng, qs = ("sync", "qi") if kt % 2 == 0 else ("gpsimd", "qo")
            E(qeng, (lambda k2, T2, c3, t3: (lambda eng: eng.dma_start(
                out=AR2[:, k2 * 512:(k2 + 1) * 512],
                in_=hqspill[c3, T2][:, t3 * 512:(t3 + 1) * 512])))(kt, Tc, cc2, ct2),
              waits=wl, inc=qs, inc_by=16)
        ddo = NOW("qo")
        wl_x2 = [("v", NOW("v"))] if Tc < 2 else [("v", res_hist[8 * (Tc - 2) + 7])]
        if Tc % 2 == 0:
            E("sync", (lambda T2: (lambda eng: eng.dma_start(
                out=SCR[:, 0:KT * 512].rearrange("p (d j) -> p d j", j=512),
                in_=x2spill[:, T2 * 512:(T2 + 1) * 512]
                .rearrange("(d p) j -> p d j", p=P))))(Tc),
              waits=wl_x2, inc="qi", inc_by=16)
        else:
            E("sync", (lambda T2: (lambda eng: eng.dma_start(
                out=RREP[:, 0:4 * 512].rearrange("p (d j) -> p d j", j=512),
                in_=x2spill[0:512, T2 * 512:(T2 + 1) * 512]
                .rearrange("(d p) j -> p d j", p=P))))(Tc),
              waits=wl_x2, inc="qi", inc_by=16)
            E("sync", (lambda T2: (lambda eng: eng.dma_start(
                out=MUREP[:, 0:4 * 512].rearrange("p (d j) -> p d j", j=512),
                in_=x2spill[512:1024, T2 * 512:(T2 + 1) * 512]
                .rearrange("(d p) j -> p d j", p=P))))(Tc),
              waits=[("qi", NOW("qi"))], inc="qi", inc_by=16)
        dd = NOW("qi")
        for m in range(KT):
            ps = pAB[m % 2]
            j = Tc * KT + m

            def mk_down(m2, ps_):
                def body(eng):
                    inst = None
                    for kt in range(IKT):
                        inst = eng.matmul(ps_[:],
                                          wd_slice_ap(kt)[:, m2 * P:(m2 + 1) * P],
                                          AR2[:, kt * 512:(kt + 1) * 512],
                                          start=(kt == 0), stop=(kt == IKT - 1))
                    return inst
                return body
            wl_mm = [("qi", dd), ("qi", qi_wd), ("qo", ddo)]
            if j >= 2:
                wl_mm.append(("v", res_hist[j - 2]))
            E("tensor", mk_down(m, ps), waits=wl_mm, inc="t")
            td = NOW("t")
            E("vector", (lambda m2, ps_, T2: (lambda eng: eng.scalar_tensor_tensor(
                AR2[:, 11264 + m2 * 512:11264 + (m2 + 1) * 512], ps_[:],
                C_der[:, 24 + m2:25 + m2], x2slice(T2, m2),
                OP.mult, OP.add)))(m, ps, Tc), waits=[("t", td)], inc="v")
            res_hist.append(NOW("v"))
        ttc_hist.append(NOW("t"))
        v_out = NOW("v")
        E("gpsimd", (lambda T2: (lambda eng: eng.dma_start(
            out=outT[:, T2 * 512:(T2 + 1) * 512].rearrange("(d p) j -> p d j", p=P),
            in_=AR2[:, 11264:11264 + KT * 512].rearrange("p (d j) -> p d j", j=512))))(Tc),
          waits=[("v", v_out)], inc="qo", inc_by=16)
        E("vector", None, waits=[("qo", NOW("qo"))], inc="v")
    qo_fin = NOW("qo")

    E("gpsimd", None, waits=[("qo", qo_fin)])

    with (
        nc.semaphore("sem_qi") as h_qi,
        nc.semaphore("sem_qo") as h_qo,
        nc.semaphore("sem_t") as h_t,
        nc.semaphore("sem_v") as h_v,
        nc.semaphore("sem_s") as h_s,
        nc.semaphore("sem_g") as h_g,
        nc.Block() as block,
    ):
        sems = {"qi": h_qi, "qo": h_qo, "t": h_t, "v": h_v, "s": h_s, "g": h_g}

        def runner(name):
            def run(eng):
                for e, fn in pr.script:
                    if e == name:
                        fn(eng, sems)
            return run
        block.sync(runner("sync"))
        block.tensor(runner("tensor"))
        block.vector(runner("vector"))
        block.scalar(runner("scalar"))
        block.gpsimd(runner("gpsimd"))

    for g in reversed(guards):
        g.__exit__(None, None, None)
    return nc


def _host_layouts(ins):
    fp = np.float32

    def tile_fm(a):
        K, M = a.shape
        return np.ascontiguousarray(
            a.reshape(K // P, P, M).transpose(1, 0, 2).reshape(P, -1)).astype(fp)

    attn_W = np.asarray(ins["L0_attn_attn_W"], fp)
    proj_W = np.asarray(ins["L0_attn_proj_W"], fp)
    freq_W = np.asarray(ins["L0_attn_freq_W"], fp)
    attn_mk = np.ascontiguousarray(
        attn_W.reshape(KT, P, 16, P).transpose(2, 1, 0, 3)
        .reshape(16, P, KT * P).transpose(1, 0, 2).reshape(P, 16 * KT * P)).astype(fp)
    proj_mk = np.ascontiguousarray(
        proj_W.reshape(KT, P, KT, P).transpose(2, 1, 0, 3)
        .reshape(KT, P, KT * P).transpose(1, 0, 2).reshape(P, KT * KT * P)).astype(fp)
    d = dict(
        attn_mk=attn_mk,
        proj_mk=proj_mk,
        wqT_t=tile_fm(np.ascontiguousarray(attn_W[:, :C].T)),
        wfs_t=tile_fm(np.concatenate([freq_W[:C], freq_W[C:]], axis=1)),
        outw=np.ascontiguousarray(np.asarray(ins["L0_attn_out_W"], fp)),
        wmod_t=tile_fm(np.asarray(ins["L0_wmod"], fp)),
        wg_nat=np.ascontiguousarray(np.asarray(ins["L0_ffn_w_gate"], fp)),
        wu_nat=np.ascontiguousarray(np.asarray(ins["L0_ffn_w_up"], fp)),
        wd_nat=np.ascontiguousarray(np.asarray(ins["L0_ffn_w_down"], fp)),
        attb=np.ascontiguousarray(np.asarray(ins["L0_attn_attn_b"], fp).reshape(16, P).T),
        outb=np.ascontiguousarray(np.asarray(ins["L0_attn_out_b"], fp).reshape(KT, P).T),
        projb=np.ascontiguousarray(np.asarray(ins["L0_attn_proj_b"], fp).reshape(KT, P).T),
        fbcol=np.ascontiguousarray(np.asarray(ins["L0_attn_freq_b"], fp).reshape(NF, 1)),
        idcol=np.ascontiguousarray(np.asarray(ins["L0_attn_identity"], fp).reshape(KT, P).T),
        eye=np.eye(P, dtype=fp),
    )
    import ml_dtypes
    bf = ml_dtypes.bfloat16
    selm16 = np.zeros((16, T), np.float32)
    for c in range(NTT):
        selm16[c, c * P:(c + 1) * P] = 1.0
    selrowa = np.zeros((16, C), np.float32)
    selrowb = np.zeros((16, C), np.float32)
    for c in range(KT):
        selrowa[c, c * P:(c + 1) * P] = 1.0
        selrowb[8 + c, c * P:(c + 1) * P] = 1.0
    d["selm16"] = selm16.astype(bf)
    d["selrowa"] = selrowa.astype(bf)
    d["selrowb"] = selrowb.astype(bf)
    return d


def kernel(**inputs):
    x = np.ascontiguousarray(np.asarray(inputs["L0_attn_x"], np.float32))
    shared = _host_layouts(inputs)
    c_in = np.asarray(inputs["L0_c"], np.float32)

    nc = build_program()
    in_maps = []
    for core in range(8):
        b, p = core // 2, core % 2
        m = dict(shared)
        m["xT"] = np.ascontiguousarray(x[b, p::2, :].T)
        if p == 0:
            xprev = np.concatenate([np.zeros((1, C), np.float32),
                                    x[b, 1::2, :][:T - 1]], axis=0)
            ev = 1.0
        else:
            xprev = x[b, 0::2, :]
            ev = 0.0
        m["xprevT"] = np.ascontiguousarray(xprev.T)
        m["c_col"] = np.ascontiguousarray(c_in[b].reshape(KT, P).T)
        m["evenm"] = np.full((NF, 1), ev, np.float32)
        in_maps.append(m)

    res = run_bass_kernel_spmd(nc, in_maps, core_ids=list(range(8)))
    out = np.empty_like(x)
    for core in range(8):
        b, p = core // 2, core % 2
        out[b, p::2, :] = res.results[core]["outT"].T
    return out


# version 13 (final)
# speedup vs baseline: 1.0200x; 1.0200x over previous
import numpy as np
import concourse.bass as bass
import concourse.mybir as mybir
from concourse.bass_utils import run_bass_kernel_spmd
from concourse.alu_op_type import AluOpType

F32 = mybir.dt.float32
BF16 = mybir.dt.bfloat16
AF = mybir.ActivationFunctionType
AX = mybir.AxisListType
OP = AluOpType

P = 128
T = 2048
C = 1024
NF = 64
KT = C // P
TC = T // 512
NTT = T // P
I = 2816
IKT = I // P
EPS_LN = 1e-5
EPS_MM = 1e-6
PI = float(np.pi)
C1H = float(np.float32(2.0 * np.pi)) / 2.0
C2H = float(np.float32(2.0 * np.pi - np.float64(np.float32(2.0 * np.pi)))) / 2.0
MIDS = [0.25, 0.75, 1.25, 1.75, 2.5, 3.5, 5.0]
NLS = [1, 2, 4, 8, 16, 32, 64, 128, 256, 512, 1024]
STOP_AFTER = None
DBG_SPILL = False
NO_QUANT = False


class _StopBuild(Exception):
    pass


class Prog:
    def __init__(self):
        self.script = []
        self.cnt = {n: 0 for n in ["qi", "qo", "t", "v", "s", "g"]}

    def now(self, sem):
        return self.cnt[sem]

    def emit(self, eng_name, body, waits=(), inc=None, inc_by=1):
        if getattr(self, "stopped", False):
            return
        if inc is not None:
            self.cnt[inc] += inc_by
        waits = tuple(waits)
        inc_t, n_inc = inc, inc_by

        def fn(eng, sems):
            for sem, val in waits:
                if val > 0:
                    eng.wait_ge(sems[sem], val)
            inst = body(eng) if body is not None else None
            if inc_t is not None:
                if inst is None:
                    inst = eng.nop()
                inst.then_inc(sems[inc_t], n_inc)
        self.script.append((eng_name, fn))


def build_program():
    nc = bass.Bass("TRN2", target_bir_lowering=False, debug=False)
    pr = Prog()
    E, NOW = pr.emit, pr.now

    def _ckpt(name, dumps):
        if STOP_AFTER != name:
            return
        barrier = [(s, NOW(s)) for s in ("t", "v", "s", "qi", "qo", "g")]
        first = True
        for ap, row0 in dumps:
            E("gpsimd", (lambda a_, r_: (lambda eng: eng.dma_start(
                out=outT[r_:r_ + a_.shape[0], 0:a_.shape[1]], in_=a_)))(ap, row0),
              waits=(barrier if first else []), inc="qo", inc_by=16)
            first = False
        E("gpsimd", None, waits=[("qo", NOW("qo"))])
        pr.stopped = True

    def din(name, shape, dtype=F32):
        return nc.dram_tensor(name, shape, dtype, kind="ExternalInput").ap()

    xT = din("xT", [C, T])
    xprevT = din("xprevT", [C, T])
    attn_mk = din("attn_mk", [P, 16 * C])
    proj_mk = din("proj_mk", [P, KT * C])
    wqT_t = din("wqT_t", [P, KT * C])
    wfs_t = din("wfs_t", [P, KT * P])
    outw = din("outw", [P, C])
    wmod_t = din("wmod_t", [P, KT * 6 * C])
    wg_nat = din("wg_nat", [I, C])
    wu_nat = din("wu_nat", [I, C])
    wd_nat = din("wd_nat", [C, I])
    attb = din("attb", [P, 16])
    outb = din("outb", [P, KT])
    projb = din("projb", [P, KT])
    fbcol = din("fbcol", [NF, 1])
    idcol = din("idcol", [P, KT])
    c_col = din("c_col", [P, KT])
    evenm = din("evenm", [NF, 1])
    eye = din("eye", [P, P])
    selm16 = din("selm16", [16, T], BF16)
    selrowa = din("selrowa", [16, C], BF16)
    selrowb = din("selrowb", [16, C], BF16)

    outT = nc.dram_tensor("outT", [C, T], F32, kind="ExternalOutput").ap()

    spk = "ExternalOutput" if DBG_SPILL else "Internal"
    vspill = nc.dram_tensor("vspill", [C, T], F32, kind="Internal").ap()
    x2spill = nc.dram_tensor("x2spill", [C, T], F32, kind="Internal").ap()
    wgspill = nc.dram_tensor("wgspill", [IKT, KT, P, P], F32, kind=spk).ap()
    wuspill = nc.dram_tensor("wuspill", [IKT, KT, P, P], F32, kind=spk).ap()
    wdspill = nc.dram_tensor("wdspill", [KT, IKT, P, P], F32, kind=spk).ap()
    hqspill = nc.dram_tensor("hqspill", [6, TC, P, 4 * 512], F32, kind=spk).ap()

    guards = []

    def S(name, shape, dtype=F32):
        g = nc.sbuf_tensor(name, shape, dtype)
        t = g.__enter__()
        guards.append(g)
        return t

    AR1 = S("AR1", [P, 8 * T])
    AR2 = S("AR2", [P, 8 * T])
    SCR = S("SCR", [P, 6144])
    STRM = S("STRM", [P, 4096])
    C_wfs = S("C_wfs", [P, KT * P])
    C_outw = S("C_outw", [P, C])
    C_eye = S("C_eye", [P, P])
    C_wqf = S("C_wqf", [P, KT * NF])
    C_attb = S("C_attb", [P, 16])
    C_outb = S("C_outb", [P, KT])
    C_modv = S("C_modv", [P, 48])
    C_der = S("C_der", [P, 32])
    C_sil = S("C_sil", [P, KT])
    C_misc = S("C_misc", [P, 16])
    C_svec = S("C_svec", [P, 16])
    C_cvec = S("C_cvec", [P, 16])
    C_nf = S("C_nf", [NF, 8])
    C_selm16 = S("C_selm16", [16, T], BF16)
    C_selrowa = S("C_selrowa", [16, C], BF16)
    C_selrowb = S("C_selrowb", [16, C], BF16)
    C_ones16 = S("C_ones16", [16, P])
    MTM = S("MTM", [P, 128])
    MTT = S("MTT", [16, P])
    RREP = S("RREP", [P, T])
    MUREP = S("MUREP", [P, T])

    pg_ = nc.psum_tensor("pG", [P, T], F32); pG = pg_.__enter__(); guards.append(pg_)
    pa_ = nc.psum_tensor("pA", [P, 512], F32); pA = pa_.__enter__(); guards.append(pa_)
    pb_ = nc.psum_tensor("pB", [P, 512], F32); pB = pb_.__enter__(); guards.append(pb_)
    pt0_ = nc.psum_tensor("pT0", [P, P], F32); pT0 = pt0_.__enter__(); guards.append(pt0_)
    pt1_ = nc.psum_tensor("pT1", [P, P], F32); pT1 = pt1_.__enter__(); guards.append(pt1_)
    pAB = [pA, pB]
    pTT = [pT0, pT1]

    def ar1(slot, rows=P):
        return AR1[0:rows, slot * T:(slot + 1) * T]

    def ar2(slot, rows=P):
        return AR2[0:rows, slot * T:(slot + 1) * T]

    RHSM = SCR[0:16, 2048:4096]

    for dst, src in [(C_wfs, wfs_t), (C_outw, outw), (C_eye, eye),
                     (C_attb, attb), (C_outb, outb), (C_sil, c_col),
                     (C_selm16, selm16), (C_selrowa, selrowa), (C_selrowb, selrowb)]:
        E("sync", (lambda d, s_: (lambda eng: eng.dma_start(out=d[:], in_=s_)))(dst, src),
          inc="qi", inc_by=16)
    E("sync", lambda eng: eng.dma_start(out=C_misc[:, 0:8], in_=projb), inc="qi", inc_by=16)
    E("sync", lambda eng: eng.dma_start(out=C_svec[:, 0:8], in_=idcol), inc="qi", inc_by=16)
    E("sync", lambda eng: eng.dma_start(out=C_nf[:, 3:4], in_=fbcol), inc="qi", inc_by=16)
    E("sync", lambda eng: eng.dma_start(out=C_nf[:, 4:5], in_=evenm), inc="qi", inc_by=16)
    pre_dma = NOW("qi")

    def mk_consts(eng):
        eng.memset(C_ones16[:], 1.0)
        eng.memset(C_misc[:, 8:9], 1.0)
        eng.memset(C_misc[:, 11:12], -PI)
        eng.memset(C_misc[:, 12:13], PI)
        return None
    E("vector", mk_consts, inc="v")
    v_consts = NOW("v")

    E("scalar", lambda eng: eng.activation(C_sil[:], C_sil[:], AF.Silu),
      waits=[("qi", pre_dma)], inc="s")
    s_sil = NOW("s")

    for k in range(KT):
        sl = (k % 2) * 6 * C
        qeng, qs = ("sync", "qi") if k % 2 == 0 else ("gpsimd", "qo")
        wl = [("t", NOW("t"))] if k >= 2 else []
        E(qeng, (lambda kk, s_: (lambda eng: eng.dma_start(
            out=AR2[:, s_:s_ + 6 * C], in_=wmod_t[:, kk * 6 * C:(kk + 1) * 6 * C])))(k, sl),
          waits=wl, inc=qs, inc_by=16)
        dd = NOW(qs)

        def mk_mod(kk, s_):
            def body(eng):
                inst = None
                for n in range(12):
                    rb = 32 * (n // 4)
                    co = (n % 4) * 512
                    inst = eng.matmul(pG[rb:rb + 1, co:co + 512],
                                      C_sil[:, kk:kk + 1],
                                      AR2[:, s_ + n * 512:s_ + (n + 1) * 512],
                                      start=(kk == 0), stop=(kk == KT - 1))
                return inst
            return body
        E("tensor", mk_mod(k, sl), waits=[(qs, dd), ("s", s_sil)], inc="t")
    t_mod = NOW("t")

    def mk_modrows(eng):
        for r in range(3):
            eng.tensor_copy(SCR[32 * r:32 * r + 1, 0:2048], pG[32 * r:32 * r + 1, :])
        return eng.drain()
    E("vector", mk_modrows, waits=[("t", t_mod)], inc="v")
    v_mr = NOW("v")

    def mk_modtr(eng):
        inst = None
        for s in range(48):
            rb = 32 * (s // 16)
            co = (s % 16) * P
            inst = eng.matmul(pT1[:, s:s + 1], SCR[rb:rb + 1, co:co + P],
                              C_eye[rb:rb + 1, rb:rb + 1], is_transpose=True,
                              start=(s == 0), stop=(s == 47))
        return inst
    E("tensor", mk_modtr, waits=[("v", v_mr)], inc="t")
    E("tensor", lambda eng: eng.matmul(pT0[0:1, 0:1], C_eye[:, 0:1], C_eye[:, 0:1],
                                       start=True, stop=True), inc="t")
    t_mtr = NOW("t")
    E("vector", lambda eng: eng.tensor_copy(C_modv[:], pT1[:, 0:48]),
      waits=[("t", t_mtr)], inc="v")
    v_mod = NOW("v")

    def mk_derived(eng):
        eng.tensor_scalar(C_der[:, 0:8], C_modv[:, 8:16], 1.0, None, OP.add)
        eng.tensor_scalar(C_der[:, 8:16], C_modv[:, 32:40], 1.0, None, OP.add)
        for src, dst in [(16, 16), (40, 24)]:
            g = C_modv[:, src:src + 8]
            eng.scalar_tensor_tensor(MTM[:, 0:8], g, -1.0, g, OP.mult, OP.max)
            eng.drain()
            eng.tensor_scalar(MTM[:, 0:8], MTM[:, 0:8], 0.25, None, OP.max)
            eng.tensor_single_scalar(MTM[:, 8:16], g, 0.0, OP.is_ge)
            eng.tensor_single_scalar(MTM[:, 16:24], g, 0.0, OP.is_lt)
            eng.drain()
            eng.tensor_tensor(MTM[:, 8:16], MTM[:, 8:16], MTM[:, 16:24], OP.subtract)
            eng.drain()
            eng.tensor_tensor(C_der[:, dst:dst + 8], MTM[:, 0:8], MTM[:, 8:16], OP.mult)
        eng.drain()
        return eng.tensor_tensor(C_misc[:, 0:8], C_misc[:, 0:8], C_der[:, 16:24], OP.mult)
    E("vector", mk_derived, waits=[("v", v_mod), ("qi", pre_dma)], inc="v")
    v_der = NOW("v")

    for k in range(KT):
        sl = (k % 2) * C
        wl = [("t", NOW("t"))] if k >= 2 else []
        E("sync", (lambda kk, s_: (lambda eng: eng.dma_start(
            out=STRM[:, s_:s_ + C], in_=wqT_t[:, kk * C:(kk + 1) * C])))(k, sl),
          waits=wl, inc="qi", inc_by=16)
        dd = NOW("qi")

        def mk_wqf(kk, s_):
            def body(eng):
                inst = None
                for m in range(KT):
                    inst = eng.matmul(pG[:, 1024 + m * NF:1024 + (m + 1) * NF],
                                      STRM[:, s_ + m * P:s_ + (m + 1) * P],
                                      C_wfs[:, kk * P:kk * P + NF],
                                      start=(kk == 0 and m == 0),
                                      stop=(kk == KT - 1 and m == KT - 1))
                return inst
            return body
        E("tensor", mk_wqf(k, sl), waits=[("qi", dd), ("v", v_mod)], inc="t")
    t_wqf = NOW("t")
    E("vector", lambda eng: eng.tensor_copy(C_wqf[:], pG[:, 1024:1024 + KT * NF]),
      waits=[("t", t_wqf)], inc="v")
    v_wqf = NOW("v")

    def mk_small(eng):
        inst = None
        for k in range(KT):
            inst = eng.matmul(pT0[0:NF, 0:1], C_wqf[:, k * NF:(k + 1) * NF],
                              C_der[:, k:k + 1], start=(k == 0), stop=False)
        for k in range(KT):
            inst = eng.matmul(pT0[0:NF, 1:2], C_wqf[:, k * NF:(k + 1) * NF],
                              C_modv[:, k:k + 1], start=False, stop=False)
        for k in range(KT):
            inst = eng.matmul(pT0[0:NF, 1:2], C_wfs[:, k * P:k * P + NF],
                              C_attb[:, k:k + 1], start=False, stop=False)
        for k in range(KT):
            inst = eng.matmul(pT0[0:NF, 2:3], C_wfs[:, k * P:k * P + NF],
                              C_svec[:, k:k + 1], start=False,
                              stop=(k == KT - 1))
        return inst
    E("tensor", mk_small, waits=[("v", v_der), ("v", v_wqf)], inc="t")
    t_small = NOW("t")

    def mk_nf(eng):
        eng.tensor_copy(C_nf[:, 1:2], pT0[0:NF, 0:1])
        eng.tensor_copy(C_nf[:, 2:3], pT0[0:NF, 1:2])
        eng.tensor_copy(C_nf[:, 0:1], pT0[0:NF, 2:3])
        eng.drain()
        eng.tensor_tensor(C_nf[:, 5:6], C_nf[:, 0:1], C_nf[:, 2:3], OP.subtract)
        eng.drain()
        eng.tensor_tensor(C_nf[:, 5:6], C_nf[:, 5:6], C_nf[:, 4:5], OP.mult)
        eng.tensor_tensor(C_nf[:, 6:7], C_nf[:, 2:3], C_nf[:, 3:4], OP.add)
        eng.drain()
        eng.tensor_scalar(C_nf[:, 7:8], C_nf[:, 6:7], PI / 2, None, OP.add)
        eng.drain()
        return eng.tensor_scalar(C_misc[0:NF, 10:11], C_nf[:, 3:4], PI / 2, None, OP.add)
    E("vector", mk_nf, waits=[("t", t_small), ("v", v_consts)], inc="v")
    v_nf = NOW("v")
    _ckpt("prelude", [(C_modv[:], 0), (C_der[:], 128), (C_wqf[:], 256),
                      (C_nf[:, 0:8], 384), (C_sil[:], 512)])

    def tr_batch(srcs, wait_list, off=0):
        n = len(srcs)

        def body(eng):
            inst = None
            for i, s_ in enumerate(srcs):
                first = (i % 4 == 0)
                last = (i % 4 == 3) or (i == n - 1)
                inst = eng.matmul(pG[:, (off + i) * P:(off + i + 1) * P], s_, C_eye[:],
                                  is_transpose=True, start=first, stop=last)
            return inst
        E("tensor", body, waits=wait_list, inc="t")
        E("tensor", lambda eng: eng.matmul(pT0[0:1, 0:1], C_eye[:, 0:1], C_eye[:, 0:1],
                                           start=True, stop=True), inc="t")
        return NOW("t")

    def stats_to_tm(wait_v, srow):
        def body(eng):
            inst = None
            for c2 in range(NTT):
                first = (c2 % 4 == 0)
                last = (c2 % 4 == 3)
                eng.matmul(pG[:, c2 * P:c2 * P + 1], SROW0[:, c2 * P:(c2 + 1) * P],
                           C_eye[0:1, 0:1], is_transpose=True,
                           start=first, stop=False)
                inst = eng.matmul(pG[:, c2 * P + 1:c2 * P + 2],
                                  SROW1[:, c2 * P:(c2 + 1) * P],
                                  C_eye[32:33, 32:33], is_transpose=True,
                                  start=False, stop=last)
            return inst
        E("tensor", body, waits=[("v", wait_v)], inc="t")
        E("tensor", lambda eng: eng.matmul(pT0[0:1, 0:1], C_eye[:, 0:1], C_eye[:, 0:1],
                                           start=True, stop=True), inc="t")
        tg = NOW("t")
        E("vector", lambda eng: eng.tensor_copy(
            MTM[:, 0:32].rearrange("p (c j) -> p c j", j=2),
            pG[:, 0:NTT * P].rearrange("p (c j) -> p c j", j=P)[:, :, 0:2]),
          waits=[("t", tg)], inc="v")
        return NOW("v")

    def ln_smallmath(wait_v):
        def body(eng):
            mu = MTM[:, 0:32:2]
            s2 = MTM[:, 1:32:2]
            eng.drain()
            eng.tensor_scalar(mu, mu, 1.0 / C, None, OP.mult)
            eng.tensor_scalar(s2, s2, 1.0 / C, None, OP.mult)
            eng.drain()
            eng.tensor_tensor(MTM[:, 32:48], mu, mu, OP.mult)
            eng.drain()
            eng.tensor_tensor(s2, s2, MTM[:, 32:48], OP.subtract)
            eng.drain()
            return eng.tensor_scalar(s2, s2, EPS_LN, None, OP.add)
        E("vector", body, waits=[("v", wait_v)], inc="v")
        v1 = NOW("v")
        E("scalar", lambda eng: eng.activation(MTM[:, 32:48], MTM[:, 1:32:2], AF.Sqrt),
          waits=[("v", v1)], inc="s")
        s1 = NOW("s")

        def body2(eng):
            r0 = MTM[:, 48:64]
            eng.reciprocal(r0, MTM[:, 32:48])
            eng.drain()
            eng.tensor_tensor(MTM[:, 64:80], r0, r0, OP.mult)
            eng.drain()
            eng.tensor_tensor(MTM[:, 64:80], MTM[:, 64:80], MTM[:, 1:32:2], OP.mult)
            eng.drain()
            eng.tensor_scalar(MTM[:, 64:80], MTM[:, 64:80], -0.5, None, OP.mult)
            eng.drain()
            eng.tensor_scalar(MTM[:, 64:80], MTM[:, 64:80], 1.5, None, OP.add)
            eng.drain()
            eng.tensor_tensor(MTM[:, 32:48], r0, MTM[:, 64:80], OP.mult)
            eng.drain()
            eng.tensor_tensor(MTM[:, 48:64], MTM[:, 0:32:2], MTM[:, 32:48], OP.mult)
            eng.drain()
            return eng.tensor_scalar(MTM[:, 48:64], MTM[:, 48:64], -1.0, None, OP.mult)
        E("vector", body2, waits=[("s", s1)], inc="v")
        return NOW("v")

    def replicate16(src_cols, wait_v, dest):
        E("tensor", (lambda sc: (lambda eng: eng.matmul(
            pT0[0:16, :], MTM[:, sc:sc + 16], C_eye[:],
            is_transpose=True, start=True, stop=True)))(src_cols),
          waits=[("v", wait_v)], inc="t")
        E("tensor", lambda eng: eng.matmul(pG[0:1, 0:1], C_eye[:, 0:1], C_eye[:, 0:1],
                                           start=True, stop=True), inc="t")
        t1 = NOW("t")

        def mk_rhs(eng):
            eng.tensor_copy(MTT[:], pT0[0:16, :])
            eng.drain()
            bc = MTT[:].unsqueeze(1).broadcast_to([16, NTT, P])
            return eng.tensor_tensor(RHSM.rearrange("k (c j) -> k c j", j=P),
                                     C_selm16[:].rearrange("k (c j) -> k c j", j=P),
                                     bc, OP.mult)
        E("vector", mk_rhs, waits=[("t", t1)], inc="v")
        v2 = NOW("v")

        def mk_rep(eng):
            inst = None
            for c in range(TC):
                inst = eng.matmul(pG[:, c * 512:(c + 1) * 512], C_ones16[:],
                                  RHSM[:, c * 512:(c + 1) * 512], start=True, stop=True)
            return inst
        E("tensor", mk_rep, waits=[("v", v2)], inc="t")
        ta = NOW("t")
        E("scalar", (lambda d: (lambda eng: (eng.activation(d[:], pG[:, 0:T], AF.Copy),
                                              eng.drain())[1]))(dest),
          waits=[("t", ta)], inc="s")
        E("vector", None, waits=[("s", NOW("s"))], inc="v")
        return NOW("v")

    def quant_tile(eng, src, dst, w, scratch, smalls):
        if NO_QUANT:
            if dst is not src:
                return eng.tensor_copy(dst, src)
            return eng.engine_nop()
        nb = w // 16
        nbw = max(nb, NF)
        bm = smalls[:, 0:nbw]
        inv = smalls[:, nbw:2 * nbw]
        eng.tensor_reduce(bm[:, 0:nb].unsqueeze(2),
                          src.rearrange("p (b i) -> p b i", i=16),
                          AX.X, OP.max, apply_absolute_value=True)
        if nb < nbw:
            eng.drain()
        eng.tensor_scalar(bm, bm, float(np.float32(1.0) / np.float32(6.0)), None, OP.mult)
        eng.tensor_scalar(bm, bm, 1e-12, None, OP.max)
        eng.reciprocal(inv, bm)
        eng.tensor_scalar(bm, bm, 0.5, None, OP.mult)
        inst = None
        for o in range(0, w, 512):
            ww = min(512, w - o)
            nbo, nbn = o // 16, ww // 16
            s_ = src[:, o:o + ww]
            d_ = dst[:, o:o + ww]
            a_ = scratch[:, 0:ww]
            g1 = scratch[:, 512:512 + ww]
            g2 = scratch[:, 1024:1024 + ww]
            tmp = scratch[:, 1536:1536 + ww]
            invb = inv[:, nbo:nbo + nbn].unsqueeze(2).broadcast_to([P, nbn, 16])
            bmb = bm[:, nbo:nbo + nbn].unsqueeze(2).broadcast_to([P, nbn, 16])
            eng.scalar_tensor_tensor(a_, s_, -1.0, s_, OP.mult, OP.max)
            eng.tensor_tensor(a_.rearrange("p (b i) -> p b i", i=16),
                              a_.rearrange("p (b i) -> p b i", i=16), invb, OP.mult)
            eng.tensor_single_scalar(g1, a_, MIDS[0], OP.is_gt)
            for mid in MIDS[1:4]:
                eng.tensor_single_scalar(tmp, a_, mid, OP.is_gt)
                eng.tensor_tensor(g1, g1, tmp, OP.add)
            eng.tensor_single_scalar(g2, a_, MIDS[4], OP.is_gt)
            eng.tensor_single_scalar(tmp, a_, MIDS[5], OP.is_gt)
            eng.tensor_tensor(g2, g2, tmp, OP.add)
            eng.tensor_single_scalar(tmp, a_, MIDS[6], OP.is_gt)
            eng.scalar_tensor_tensor(g2, tmp, 2.0, g2, OP.mult, OP.add)
            eng.scalar_tensor_tensor(g1, g2, 2.0, g1, OP.mult, OP.add)
            eng.tensor_tensor(g1.rearrange("p (b i) -> p b i", i=16),
                              g1.rearrange("p (b i) -> p b i", i=16), bmb, OP.mult)
            eng.tensor_single_scalar(tmp, s_, 0.0, OP.is_ge)
            eng.tensor_single_scalar(g2, s_, 0.0, OP.is_lt)
            eng.tensor_tensor(tmp, tmp, g2, OP.subtract)
            inst = eng.tensor_tensor(d_, g1, tmp, OP.mult)
        return inst

    def quant_chunks(src, w, regs, smalls, wl_first, state, sign_dve=False):
        a_r, tmp_r, g2_r, g1a, g1b, w1_r = regs
        nb = w // 16
        nbw = max(nb, NF)
        bm = smalls[:, 0:nbw]
        inv = smalls[:, nbw:2 * nbw]
        nchunks = (w + 511) // 512
        if NO_QUANT:
            def emit_nop(ci):
                if ci == 0:
                    E("vector", None, waits=wl_first, inc="v")
                return ("v", NOW("v"))
            return nchunks, emit_nop

        def emit_chunk(ci):
            o = ci * 512
            ww = min(512, w - o)
            nbo, nbn = o // 16, ww // 16
            s_ = src[:, o:o + ww]
            g1 = (g1a if ci % 2 == 0 else g1b)[:, 0:ww]
            a2 = a_r[:, 0:ww]
            tmp = tmp_r[:, 0:ww]
            g2 = g2_r[:, 0:ww]
            invb = inv[:, nbo:nbo + nbn].unsqueeze(2).broadcast_to([P, nbn, 16])
            bmb = bm[:, nbo:nbo + nbn].unsqueeze(2).broadcast_to([P, nbn, 16])
            par = "A" if ci % 2 == 0 else "B"

            def dve_body(first, s2, a3, t3, g23, g13, invb_):
                def body(eng):
                    if first:
                        eng.tensor_reduce(bm[:, 0:nb].unsqueeze(2),
                                          src.rearrange("p (b i) -> p b i", i=16),
                                          AX.X, OP.max, apply_absolute_value=True)
                        if nb < nbw:
                            eng.drain()
                        eng.tensor_scalar(bm, bm,
                                          float(np.float32(1.0) / np.float32(6.0)),
                                          None, OP.mult)
                        eng.tensor_scalar(bm, bm, 1e-12, None, OP.max)
                        eng.reciprocal(inv, bm)
                        eng.tensor_scalar(bm, bm, 0.5, None, OP.mult)
                    eng.scalar_tensor_tensor(a3, s2, -1.0, s2, OP.mult, OP.max)
                    eng.tensor_tensor(a3.rearrange("p (b i) -> p b i", i=16),
                                      a3.rearrange("p (b i) -> p b i", i=16),
                                      invb_, OP.mult)
                    eng.tensor_single_scalar(g13, a3, MIDS[0], OP.is_gt)
                    for mid in MIDS[1:4]:
                        eng.tensor_single_scalar(t3, a3, mid, OP.is_gt)
                        eng.tensor_tensor(g13, g13, t3, OP.add)
                    eng.tensor_single_scalar(g23, a3, MIDS[4], OP.is_gt)
                    eng.tensor_single_scalar(t3, a3, MIDS[5], OP.is_gt)
                    eng.tensor_tensor(g23, g23, t3, OP.add)
                    eng.tensor_single_scalar(t3, a3, MIDS[6], OP.is_gt)
                    eng.scalar_tensor_tensor(g23, t3, 2.0, g23, OP.mult, OP.add)
                    inst = eng.scalar_tensor_tensor(g13, g23, 2.0, g13, OP.mult, OP.add)
                    if sign_dve:
                        eng.tensor_single_scalar(t3, s2, 0.0, OP.is_ge)
                        eng.tensor_single_scalar(g23, s2, 0.0, OP.is_lt)
                        inst = eng.tensor_tensor(s2, t3, g23, OP.subtract)
                    return inst
                return body
            wl = list(wl_first) if ci == 0 else []
            gk = state.get("last") if ci == 0 else state.get(par)
            if gk is not None:
                wl.append(("g", gk))
            E("vector", dve_body(ci == 0, s_, a2, tmp, g2, g1, invb), waits=wl, inc="v")
            v_d = NOW("v")
            wl_g = [("v", v_d)]
            if not sign_dve:
                E("scalar", (lambda s2: (lambda eng: (eng.activation(s2, s2, AF.Sign),
                                                      eng.drain())[1]))(s_),
                  waits=[("v", v_d), ("s", NOW("s"))], inc="s")
                wl_g.append(("s", NOW("s")))

            def gps_body(s2, g13, bmb_, ww_):
                def body(eng):
                    eng.tensor_tensor(w1_r[:, 0:ww_].rearrange("p (b i) -> p b i", i=16),
                                      g13.rearrange("p (b i) -> p b i", i=16),
                                      bmb_, OP.mult)
                    return eng.tensor_tensor(s2, s2, w1_r[:, 0:ww_], OP.mult)
                return body
            E("gpsimd", gps_body(s_, g1, bmb, ww), waits=wl_g, inc="g")
            state[par] = NOW("g")
            state["last"] = NOW("g")
            return ("g", NOW("g"))
        return nchunks, emit_chunk

    def quant_emit(src, w, regs, smalls, wl_first, state, sign_dve=False):
        nchunks, emit_chunk = quant_chunks(src, w, regs, smalls, wl_first, state,
                                           sign_dve)
        ret = None
        for ci in range(nchunks):
            ret = emit_chunk(ci)
        return ret

    for k in range(KT):
        E("sync", (lambda kk: (lambda eng: eng.dma_start(
            out=ar1(kk), in_=xT[kk * P:(kk + 1) * P, :])))(k), inc="qi", inc_by=16)
    x_loaded = NOW("qi")

    tsq1_hist = []
    for k in range(KT):
        sb1 = (k % 2) * T

        def mk_sum(kk):
            def body(eng):
                inst = None
                for c in range(TC):
                    inst = eng.matmul(pG[0:1, c * 512:(c + 1) * 512], C_misc[:, 8:9],
                                      ar1(kk)[:, c * 512:(c + 1) * 512],
                                      start=(kk == 0), stop=(kk == KT - 1))
                return inst
            return body
        E("tensor", mk_sum(k), waits=[("qi", x_loaded), ("v", v_nf)], inc="t")
        wl_sq = [("qi", x_loaded), ("s", NOW("s"))]
        if k >= 2:
            wl_sq.append(("t", tsq1_hist[k - 2]))
        E("scalar", (lambda kk, s_: (lambda eng: eng.activation(
            SCR[:, s_:s_ + T], ar1(kk), AF.Square)))(k, sb1),
          waits=wl_sq, inc="s")
        s_h = NOW("s")

        def mk_sumsq(kk, s_):
            def body(eng):
                inst = None
                for c in range(TC):
                    inst = eng.matmul(pG[32:33, c * 512:(c + 1) * 512], C_misc[:, 8:9],
                                      SCR[:, s_ + c * 512:s_ + (c + 1) * 512],
                                      start=(kk == 0), stop=(kk == KT - 1))
                return inst
            return body
        E("tensor", mk_sumsq(k, sb1), waits=[("s", s_h)], inc="t")
        tsq1_hist.append(NOW("t"))
        E("scalar", (lambda kk: (lambda eng: eng.activation(
            ar1(kk), ar1(kk), AF.Copy, scale=C_der[:, kk:kk + 1])))(k),
          waits=[("t", NOW("t"))], inc="s")
    t_stats = NOW("t")
    s_xsc = NOW("s")

    SROW0 = SCR[0:1, 4096:6144]
    SROW1 = SCR[32:33, 4096:6144]
    def mk_srow(eng):
        eng.tensor_copy(SROW0, pG[0:1, :])
        return eng.tensor_copy(SROW1, pG[32:33, :])
    E("vector", mk_srow, waits=[("t", t_stats)], inc="v")
    v_tm = stats_to_tm(NOW("v"), None)
    v_ln = ln_smallmath(v_tm)
    v_r1 = replicate16(32, v_ln, RREP)
    v_r2 = replicate16(48, v_r1, MUREP)
    _ckpt("reps", [(RREP[:], 0), (MUREP[:], 128), (MTM[:], 256)])

    tq_hist = []
    ev_hist = []
    vvec_hist = []
    qo_hist = {}
    for m in range(16):
        sl = (m % 2) * C
        wl = [("t", tq_hist[m - 2])] if m >= 2 else [("v", v_r2)]
        E("sync", (lambda mm, s_: (lambda eng: eng.dma_start(
            out=STRM[:, s_:s_ + C], in_=attn_mk[:, mm * C:(mm + 1) * C])))(m, sl),
          waits=wl, inc="qi", inc_by=16)
        dd = NOW("qi")

        def mk_vecs(s_):
            def body(eng):
                inst = None
                for k in range(KT):
                    inst = eng.matmul(pT0[:, 4:5], STRM[:, s_ + k * P:s_ + (k + 1) * P],
                                      C_modv[:, k:k + 1], start=(k == 0), stop=False)
                for k in range(KT):
                    inst = eng.matmul(pT0[:, 5:6], STRM[:, s_ + k * P:s_ + (k + 1) * P],
                                      C_der[:, k:k + 1], start=False, stop=(k == KT - 1))
                return inst
            return body
        wl_vec = [("qi", dd), ("s", s_xsc)]
        if m >= 1:
            wl_vec.append(("v", vvec_hist[m - 1]))
        E("tensor", mk_vecs(sl), waits=wl_vec, inc="t")
        tv = NOW("t")
        E("vector", (lambda mm: (lambda eng: (
            eng.tensor_copy(C_svec[:, mm:mm + 1], pT0[:, 4:5]),
            eng.drain(),
            eng.tensor_tensor(C_svec[:, mm:mm + 1], C_svec[:, mm:mm + 1],
                              C_attb[:, mm:mm + 1], OP.add),
            eng.drain(),
            eng.tensor_copy(C_cvec[:, mm:mm + 1], pT0[:, 5:6]),
            eng.drain())[-2]))(m),
          waits=[("t", tv)], inc="v")
        vvec_hist.append(NOW("v"))
        for c in range(TC):
            ps = pAB[c % 2]
            j = m * TC + c

            def mk_qv(s_, cc, ps_):
                def body(eng):
                    inst = None
                    for k in range(KT):
                        inst = eng.matmul(ps_[:], STRM[:, s_ + k * P:s_ + (k + 1) * P],
                                          ar1(k)[:, cc * 512:(cc + 1) * 512],
                                          start=(k == 0), stop=(k == KT - 1))
                    return inst
                return body
            wl_mm = [("v", v_r2)]
            if j >= 2:
                wl_mm.append(("v", ev_hist[j - 2]))
            E("tensor", mk_qv(sl, c, ps), waits=wl_mm, inc="t")
            t_qv = NOW("t")

            def mk_evac(mm, cc, ps_):
                def body(eng):
                    cs = slice(cc * 512, (cc + 1) * 512)
                    if mm < KT:
                        dst = ar2(mm)[:, cs]
                    else:
                        vb = ((mm - KT) % 2) * T
                        dst = SCR[:, vb + cc * 512:vb + (cc + 1) * 512]
                    eng.tensor_tensor(dst, RREP[:, cs], ps_[:], OP.mult)
                    eng.scalar_tensor_tensor(dst, MUREP[:, cs], C_cvec[:, mm:mm + 1],
                                             dst, OP.mult, OP.add)
                    return eng.tensor_scalar(dst, dst, C_svec[:, mm:mm + 1], None, OP.add)
                return body
            wl_ev = [("t", t_qv), ("v", vvec_hist[m])]
            if c == 0 and m >= KT + 2:
                wl_ev.append(("qo", qo_hist[m - 2]))
            E("vector", mk_evac(m, c, ps), waits=wl_ev, inc="v")
            ev_hist.append(NOW("v"))
        tq_hist.append(NOW("t"))
        if m >= KT:
            vdone = NOW("v")
            E("gpsimd", (lambda mm: (lambda eng: eng.dma_start(
                out=vspill[(mm - KT) * P:(mm - KT + 1) * P, :],
                in_=SCR[:, ((mm - KT) % 2) * T:((mm - KT) % 2) * T + T])))(m),
              waits=[("v", vdone)], inc="qo", inc_by=16)
            qo_hist[m] = NOW("qo")
    t_qvall = NOW("t")
    v_qvall = NOW("v")
    E("vector", None, waits=[("qo", NOW("qo"))], inc="v")
    _ckpt("qv", [(ar2(k), 128 * k) for k in range(KT)])
    _ckpt("qvv", [(C_svec[:], 0), (C_cvec[:], 128)])

    tg_hist = []
    for k in range(KT):
        sb = (k % 2) * T
        wl = [("t", NOW("t")), ("qo", NOW("qo"))] if k < 2 else [("t", tg_hist[k - 2])]
        E("sync", (lambda kk, s_: (lambda eng: eng.dma_start(
            out=STRM[:, s_:s_ + T], in_=xprevT[kk * P:(kk + 1) * P, :])))(k, sb),
          waits=wl, inc="qi", inc_by=16)
        dd = NOW("qi")

        def mk_psum(kk, s_):
            def body(eng):
                inst = None
                for c in range(TC):
                    inst = eng.matmul(pG[64:65, c * 512:(c + 1) * 512], C_misc[:, 8:9],
                                      STRM[:, s_ + c * 512:s_ + (c + 1) * 512],
                                      start=(kk == 0), stop=(kk == KT - 1))
                return inst
            return body
        E("tensor", mk_psum(k, sb), waits=[("qi", dd)], inc="t")
        t_raw = NOW("t")
        wl_sq = [("qi", dd), ("s", NOW("s"))]
        if k == 0:
            wl_sq.append(("qo", NOW("qo")))
        E("scalar", (lambda kk, s_: (lambda eng: eng.activation(
            SCR[:, s_:s_ + T], STRM[:, s_:s_ + T], AF.Square)))(k, sb),
          waits=wl_sq, inc="s")
        s_h = NOW("s")

        SQACC = [(pA, 0), (pA, 32), (pA, 64), (pB, 0)]

        def mk_psumsq(kk, s_):
            def body(eng):
                inst = None
                for c in range(TC):
                    pp, rb = SQACC[c]
                    inst = eng.matmul(pp[rb:rb + 1, :], C_misc[:, 8:9],
                                      SCR[:, s_ + c * 512:s_ + (c + 1) * 512],
                                      start=(kk == 0), stop=(kk == KT - 1))
                return inst
            return body
        wl_sq2 = [("s", s_h)]
        if k == 0:
            wl_sq2.append(("v", v_qvall))
        E("tensor", mk_psumsq(k, sb), waits=wl_sq2, inc="t")
        E("scalar", (lambda kk, s_: (lambda eng: eng.activation(
            STRM[:, s_:s_ + T], STRM[:, s_:s_ + T], AF.Copy,
            scale=C_der[:, kk:kk + 1])))(k, sb),
          waits=[("t", t_raw)], inc="s")
        s_sc = NOW("s")

        def mk_g(kk, s_):
            def body(eng):
                inst = None
                for c in range(TC):
                    inst = eng.matmul(pG[0:NF, c * 512:(c + 1) * 512],
                                      C_wqf[:, kk * NF:(kk + 1) * NF],
                                      STRM[:, s_ + c * 512:s_ + (c + 1) * 512],
                                      start=(kk == 0), stop=(kk == KT - 1))
                return inst
            return body
        E("tensor", mk_g(k, sb), waits=[("s", s_sc)], inc="t")
        tg_hist.append(NOW("t"))
    t_g = NOW("t")

    def mk_psrow(eng):
        eng.tensor_copy(SROW0, pG[64:65, :])
        for c in range(TC):
            pp, rb = [(pA, 0), (pA, 32), (pA, 64), (pB, 0)][c]
            eng.tensor_copy(SROW1[:, c * 512:(c + 1) * 512], pp[rb:rb + 1, :])
        return eng.tensor_copy(ar1(7, NF), pG[0:NF, :])
    E("vector", mk_psrow, waits=[("t", t_g)], inc="v")
    v_psr = NOW("v")
    v_tm2 = stats_to_tm(v_psr, None)
    v_ln2 = ln_smallmath(v_tm2)
    v_rp = replicate16(32, v_ln2, RREP)
    v_mup = replicate16(48, v_rp, MUREP)

    b0_hist = []
    for c in range(TC):
        ps = pAB[c % 2]

        def mk_b0(cc, ps_):
            def body(eng):
                inst = None
                for k in range(KT):
                    inst = eng.matmul(ps_[0:NF, :], C_wfs[:, k * P + NF:(k + 1) * P],
                                      ar2(k)[:, cc * 512:(cc + 1) * 512],
                                      start=(k == 0), stop=(k == KT - 1))
                return inst
            return body
        wl_b0 = [("v", v_psr)]
        if c >= 2:
            wl_b0.append(("v", b0_hist[c - 2]))
        E("tensor", mk_b0(c, ps), waits=wl_b0, inc="t")
        tb = NOW("t")
        E("vector", (lambda cc, ps_: (lambda eng: eng.tensor_copy(
            ar1(1, NF)[:, cc * 512:(cc + 1) * 512], ps_[0:NF, :])))(c, ps),
          waits=[("t", tb)], inc="v")
        b0_hist.append(NOW("v"))

    def mk_aaux(eng):
        As = ar1(0, NF)
        Gs = ar1(7, NF)
        t_ = ar1(4, NF)
        eng.tensor_tensor(t_, C_nf[:, 1:2].broadcast_to([NF, T]), MUREP[0:NF, :], OP.mult)
        eng.tensor_tensor(As, Gs, RREP[0:NF, :], OP.mult)
        eng.tensor_tensor(As, As, t_, OP.add)
        eng.tensor_scalar(MTM[0:NF, 0:1], C_nf[:, 4:5], -1.0, None, OP.mult)
        eng.drain()
        eng.tensor_scalar(MTM[0:NF, 0:1], MTM[0:NF, 0:1], 1.0, None, OP.add)
        eng.drain()
        eng.tensor_tensor(As[:, 0:1], As[:, 0:1], MTM[0:NF, 0:1], OP.mult)
        eng.drain()
        return eng.tensor_tensor(As[:, 0:1], As[:, 0:1], C_nf[:, 5:6], OP.add)
    E("vector", mk_aaux, waits=[("v", v_mup)], inc="v")
    v_b0 = NOW("v")
    _ckpt("ab0", [(ar1(0), 0), (ar1(1), 128), (RREP[:], 256), (MUREP[:], 384),
                  (ar1(7), 512)])

    def f_assemble(nl, bias_sin, bias_cos, wait_v):
        As, Bs, F2 = ar1(0, NF), ar1(1, NF), ar1(2)

        def body(eng):
            if nl > 0:
                eng.tensor_tensor(F2[0:NF, nl:T], As[:, 0:T - nl], Bs[:, nl:T], OP.add)
                return eng.tensor_tensor(F2[0:NF, 0:nl],
                                         C_nf[:, 0:1].broadcast_to([NF, nl]),
                                         Bs[:, 0:nl], OP.add)
            return eng.tensor_tensor(F2[0:NF, :], As[:, :], Bs[:, :], OP.add)
        E("vector", body, waits=[("v", wait_v)], inc="v")
        vf = NOW("v")

        def pack(eng):
            eng.activation(F2[NF:P, :], F2[0:NF, :], AF.Identity, bias=bias_cos)
            return eng.activation(F2[0:NF, :], F2[0:NF, :], AF.Identity, bias=bias_sin)
        E("scalar", pack, waits=[("v", vf), ("s", NOW("s"))], inc="s")
        return NOW("s")

    def sin_stage(wait_s):
        F2, Sb, s1_ = ar1(2), ar1(3), ar1(4)
        E("scalar", lambda eng: eng.activation(s1_, F2[:], AF.Sign,
                                               bias=C_misc[:, 11:12]),
          waits=[("s", wait_s), ("v", NOW("v"))], inc="s")
        E("scalar", lambda eng: eng.activation(ar1(5), F2[:], AF.Sign,
                                               bias=C_misc[:, 12:13]), inc="s")
        ss = NOW("s")

        def red(eng):
            eng.tensor_tensor(s1_, s1_, ar1(5), OP.add)
            eng.scalar_tensor_tensor(F2[:], s1_, -C1H, F2[:], OP.mult, OP.add)
            return eng.scalar_tensor_tensor(F2[:], s1_, -C2H, F2[:], OP.mult, OP.add)
        E("vector", red, waits=[("s", ss)], inc="v")
        vr = NOW("v")
        E("scalar", lambda eng: eng.activation(Sb[:], F2[:], AF.Sin),
          waits=[("v", vr)], inc="s")
        return NOW("s")

    def z_stage(wait_s, guard_v):
        Sb = ar1(3)
        zev = []
        for mz in range(KT):
            for c in range(TC):
                j = mz * TC + c
                ps = pAB[j % 2]

                def mk_z(m2, c2, ps_):
                    def body(eng):
                        return eng.matmul(ps_[:], C_outw[:, m2 * P:(m2 + 1) * P],
                                          Sb[:, c2 * 512:(c2 + 1) * 512],
                                          start=True, stop=True)
                    return body
                wl = [("s", wait_s if j < 2 else zev[j - 2])]
                E("tensor", mk_z(mz, c, ps), waits=wl, inc="t")
                tz = NOW("t")
                wl_e = [("t", tz)]
                if j == 0:
                    wl_e.append(("v", guard_v))
                E("scalar", (lambda m2, c2, ps_: (lambda eng: eng.activation(
                    ar2(m2)[:, c2 * 512:(c2 + 1) * 512], ps_[:], AF.Identity,
                    bias=C_outb[:, m2:m2 + 1])))(mz, c, ps),
                  waits=wl_e, inc="s")
                zev.append(NOW("s"))
        E("scalar", lambda eng: eng.drain(), inc="s")
        return NOW("s")

    def mmnorm_stage(wait_s):
        acc = ar1(6)
        zview = AR2[:].rearrange("p (d t) -> p t d", d=KT)
        tgs = []
        for h in range(2):
            E("vector", (lambda h_: (lambda eng: eng.tensor_reduce(
                acc[:, h_ * 1024:(h_ + 1) * 1024].unsqueeze(2),
                zview[:, h_ * 1024:(h_ + 1) * 1024, :], AX.X, OP.max,
                apply_absolute_value=True)))(h),
              waits=[("s", wait_s)], inc="v")
            v0 = NOW("v")
            if h == 1:
                mmnorm_stage.v_zread = v0
            tgs.append(tr_batch([acc[:, (8 * h + c) * P:(8 * h + c + 1) * P]
                                 for c in range(8)], [("v", v0)], off=8 * h))

        def reds(h_):
            def body(eng):
                inst = None
                for c2 in range(8 * h_, 8 * h_ + 8):
                    inst = eng.tensor_reduce(MTM[:, 32 + c2:33 + c2],
                                             pG[:, c2 * P:(c2 + 1) * P], AX.X, OP.max)
                return inst
            return body
        E("vector", reds(0), waits=[("t", tgs[0])], inc="v")
        E("vector", reds(1), waits=[("t", tgs[1])], inc="v")
        vtm = NOW("v")

        def rmath(eng):
            eng.drain()
            eng.tensor_scalar(MTM[:, 32:48], MTM[:, 32:48], EPS_MM, None, OP.add)
            eng.drain()
            return eng.reciprocal(MTM[:, 32:48], MTM[:, 32:48])
        E("vector", rmath, waits=[("v", vtm)], inc="v")
        return replicate16(32, NOW("v"), RREP)

    def ab_stage(wait_v, wait_s_z):
        evs = []
        for c in range(TC):
            ps = pAB[c % 2]

            def mk_ab(cc, ps_):
                def body(eng):
                    inst = None
                    for k in range(KT):
                        inst = eng.matmul(ps_[:], C_wfs[:, k * P:(k + 1) * P],
                                          ar2(k)[:, cc * 512:(cc + 1) * 512],
                                          start=(k == 0), stop=(k == KT - 1))
                    return inst
                return body
            wl = [("s", wait_s_z)]
            if c == 0 and swp.get("vflush") is not None:
                wl.append(("v", swp["vflush"]))
            if c >= 2:
                wl.append(("v", evs[c - 2]))
            E("tensor", mk_ab(c, ps), waits=wl, inc="t")
            tab = NOW("t")

            def mk_ev(cc, ps_):
                def body(eng):
                    cs = slice(cc * 512, (cc + 1) * 512)
                    eng.tensor_tensor(ar1(0, NF)[:, cs], RREP[0:NF, cs],
                                      ps_[0:NF, :], OP.mult)
                    return eng.tensor_tensor(ar1(1, NF)[:, cs], RREP[0:NF, cs],
                                             ps_[NF:P, :], OP.mult)
                return body
            E("vector", mk_ev(cc=c, ps_=ps), waits=[("t", tab), ("v", wait_v)], inc="v")
            evs.append(NOW("v"))
        return NOW("v")

    WP_SCAN = 16
    swp = {"qi": {}, "t": {}, "st": {"wg": {}, "wu": {}}}
    swp_regs = (SCR[:, 0:512], SCR[:, 512:1024], SCR[:, 1024:1536],
                SCR[:, 1536:2048], SCR[:, 5120:5632], SCR[:, 5632:6144])

    def swp_fam(j):
        return (wg_nat, wgspill, "wg") if j % 2 == 0 else (wu_nat, wuspill, "wu")

    def swp_load(j):
        nat, _, _ = swp_fam(j)
        ch = j // 2
        wl = [("t", swp["t"][j - 2])] if j >= 2 else \
            [("v", NOW("v")), ("t", NOW("t"))]
        E("sync", (lambda n_, c2, b_: (lambda eng: eng.dma_start(
            out=STRM[:, b_ * C:(b_ + 1) * C], in_=n_[c2 * P:(c2 + 1) * P, :])))(
                nat, ch, j % 2),
          waits=wl, inc="qi", inc_by=16)
        swp["qi"][j] = NOW("qi")

    def tr_batch_ab(srcs, wait_list):
        def body(eng):
            inst = None
            for i, s_ in enumerate(srcs):
                pp = pA if i < 4 else pB
                ii = i % 4
                inst = eng.matmul(pp[:, ii * P:(ii + 1) * P], s_, C_eye[:],
                                  is_transpose=True, start=(ii == 0), stop=(ii == 3))
            return inst
        E("tensor", body, waits=wait_list, inc="t")
        E("tensor", lambda eng: eng.matmul(pT0[0:1, 0:1], C_eye[:, 0:1],
                                           C_eye[:, 0:1], start=True, stop=True),
          inc="t")
        return NOW("t")

    def swp_flush():
        if swp.get("pend") is None:
            return
        tg, spill, ch = swp.pop("pend")

        def cp(eng):
            eng.tensor_copy(SCR[:, 4096:4096 + 512], pA[:])
            return eng.tensor_copy(SCR[:, 4608:4608 + 512], pB[:])
        E("vector", cp, waits=[("t", tg)], inc="v")
        swp["vflush"] = NOW("v")
        v_st = NOW("v")
        E("sync", (lambda sp, c2: (lambda eng: eng.dma_start(
            out=sp[c2, 0:KT].rearrange("d p j -> p d j"),
            in_=SCR[:, 4096:4096 + KT * P].rearrange("p (d j) -> p d j", j=P))))(
                spill, ch),
          waits=[("v", v_st)], inc="qi", inc_by=16)
        E("vector", None, waits=[("qi", NOW("qi"))], inc="v")

    def swp_slot(j, s_z_cur):
        if j >= 2 * WP_SCAN:
            return
        swp_flush()
        if j not in swp["qi"]:
            swp_load(j)
        if j + 1 < 2 * WP_SCAN and (j + 1) not in swp["qi"]:
            swp_load(j + 1)
        nat, spill, fam = swp_fam(j)
        ch = j // 2
        src_ = STRM[:, (j % 2) * C:(j % 2) * C + C]
        wl_q = [("qi", swp["qi"][j]), ("v", NOW("v"))]
        if ch == 0:
            wl_q.append(("g", NOW("g")))
        qw = quant_emit(src_, C, swp_regs, MUREP[:, 0:128], wl_q,
                        swp["st"][fam], sign_dve=True)
        tg = tr_batch_ab([src_[:, i * P:(i + 1) * P] for i in range(KT)],
                         [qw, ("v", NOW("v")), ("s", s_z_cur)])
        swp["pend"] = (tg, spill, ch)
        swp["t"][j] = NOW("t")

    s_pack = f_assemble(0, C_nf[:, 6:7], C_nf[:, 7:8], v_b0)
    s_sin = sin_stage(s_pack)
    _ckpt("f1", [(ar1(2), 0), (ar1(3), 128)])
    s_z = z_stage(s_sin, NOW("v"))
    v_r = mmnorm_stage(s_z)
    v_zread = mmnorm_stage.v_zread
    _ckpt("scan1", [(ar2(k), 128 * k) for k in range(KT)])
    _ckpt("scan1r", [(RREP[:], 0)])
    it_no = 1
    for nl in NLS:
        it_no += 1
        v_ab = ab_stage(v_r, s_z)
        s_pack = f_assemble(nl, C_nf[:, 3:4], C_misc[0:NF, 10:11], v_ab)
        s_sin = sin_stage(s_pack)
        s_z = z_stage(s_sin, v_zread)
        swp_slot(3 * (it_no - 2), s_z)
        swp_slot(3 * (it_no - 2) + 1, s_z)
        swp_slot(3 * (it_no - 2) + 2, s_z)
        v_r = mmnorm_stage(s_z)
        v_zread = mmnorm_stage.v_zread
        swp_flush()
        _ckpt("scan%d" % it_no, [(ar2(k), 128 * k) for k in range(KT)])

    y_hist = []
    for k in range(KT):
        ss = (k % 2) * T
        wl = [("v", v_r)] if k < 2 else [("v", y_hist[k - 2])]
        E("sync", (lambda kk, s_: (lambda eng: eng.dma_start(
            out=SCR[:, s_:s_ + T], in_=vspill[kk * P:(kk + 1) * P, :])))(k, ss),
          waits=wl, inc="qi", inc_by=16)
        dd = NOW("qi")

        def mk_y(kk, s_):
            def body(eng):
                eng.tensor_tensor(ar2(kk), ar2(kk), RREP[:], OP.mult)
                return eng.tensor_tensor(ar2(kk), ar2(kk), SCR[:, s_:s_ + T], OP.mult)
            return body
        E("vector", mk_y(k, ss), waits=[("qi", dd)], inc="v")
        y_hist.append(NOW("v"))
    v_y = NOW("v")
    _ckpt("ydump", [(ar2(k), 128 * k) for k in range(KT)])

    tp_hist = []
    pe_hist = []
    for m in range(KT):
        sl = (m % 2) * C
        wl = [("v", v_y)] if m < 2 else [("t", tp_hist[m - 2])]
        E("sync", (lambda mm, s_: (lambda eng: eng.dma_start(
            out=STRM[:, s_:s_ + C], in_=proj_mk[:, mm * C:(mm + 1) * C])))(m, sl),
          waits=wl, inc="qi", inc_by=16)
        ss = (m % 2) * T
        wl2 = [("v", v_y)] if m < 2 else [("v", pe_hist[4 * (m - 2) + 3])]
        E("sync", (lambda mm, s_: (lambda eng: eng.dma_start(
            out=SCR[:, s_:s_ + T], in_=xT[mm * P:(mm + 1) * P, :])))(m, ss),
          waits=wl2, inc="qi", inc_by=16)
        dd = NOW("qi")
        E("scalar", (lambda mm, s_: (lambda eng: (eng.activation(
            SCR[:, s_:s_ + T], SCR[:, s_:s_ + T], AF.Identity,
            bias=C_misc[:, mm:mm + 1]), eng.drain())[1]))(m, ss),
          waits=[("qi", dd), ("s", NOW("s"))], inc="s")
        s_xp = NOW("s")
        for c in range(TC):
            ps = pAB[c % 2]
            j = m * TC + c

            def mk_proj(s_, cc, ps_):
                def body(eng):
                    inst = None
                    for k in range(KT):
                        inst = eng.matmul(ps_[:], STRM[:, s_ + k * P:s_ + (k + 1) * P],
                                          ar2(k)[:, cc * 512:(cc + 1) * 512],
                                          start=(k == 0), stop=(k == KT - 1))
                    return inst
                return body
            wl_mm = [("qi", dd)]
            wl_mm.append(("v", pe_hist[j - 2]) if j >= 2 else ("v", v_y))
            E("tensor", mk_proj(sl, c, ps), waits=wl_mm, inc="t")
            tp = NOW("t")
            wl_e = [("t", tp)] if c > 0 else [("t", tp), ("s", s_xp)]
            E("vector", (lambda mm, s2, cc, ps_: (lambda eng: eng.scalar_tensor_tensor(
                ar1(mm)[:, cc * 512:(cc + 1) * 512], ps_[:], C_der[:, 16 + mm:17 + mm],
                SCR[:, s2 + cc * 512:s2 + (cc + 1) * 512], OP.mult, OP.add)))(m, ss, c, ps),
              waits=wl_e, inc="v")
            pe_hist.append(NOW("v"))
        tp_hist.append(NOW("t"))
    v_x2 = NOW("v")

    _ckpt("x2", [(ar1(k), 128 * k) for k in range(KT)])
    for k in range(KT):
        E("gpsimd", (lambda kk: (lambda eng: eng.dma_start(
            out=x2spill[kk * P:(kk + 1) * P, :], in_=ar1(kk))))(k),
          waits=[("v", v_x2)], inc="qo", inc_by=16)
    qo_x2 = NOW("qo")

    tsq_hist = []
    for k in range(KT):
        sb2 = (k % 2) * T

        def mk_sum2(kk):
            def body(eng):
                inst = None
                for c in range(TC):
                    inst = eng.matmul(pG[0:1, c * 512:(c + 1) * 512], C_misc[:, 8:9],
                                      ar1(kk)[:, c * 512:(c + 1) * 512],
                                      start=(kk == 0), stop=(kk == KT - 1))
                return inst
            return body
        E("tensor", mk_sum2(k), waits=[("v", v_x2)], inc="t")
        wl_sq = [("v", v_x2), ("s", NOW("s"))]
        if k >= 2:
            wl_sq.append(("t", tsq_hist[k - 2]))
        E("scalar", (lambda kk, s_: (lambda eng: eng.activation(
            SCR[:, s_:s_ + T], ar1(kk), AF.Square)))(k, sb2),
          waits=wl_sq, inc="s")
        s_h = NOW("s")

        def mk_sumsq2(kk, s_):
            def body(eng):
                inst = None
                for c in range(TC):
                    inst = eng.matmul(pG[32:33, c * 512:(c + 1) * 512], C_misc[:, 8:9],
                                      SCR[:, s_ + c * 512:s_ + (c + 1) * 512],
                                      start=(kk == 0), stop=(kk == KT - 1))
                return inst
            return body
        E("tensor", mk_sumsq2(k, sb2), waits=[("s", s_h)], inc="t")
        tsq_hist.append(NOW("t"))
    t_st2 = NOW("t")

    def mk_srow2(eng):
        eng.tensor_copy(SROW0, pG[0:1, :])
        return eng.tensor_copy(SROW1, pG[32:33, :])
    E("vector", mk_srow2, waits=[("t", t_st2)], inc="v")
    v_tm3 = stats_to_tm(NOW("v"), None)
    v_ln3 = ln_smallmath(v_tm3)

    def mk_rowstack(eng):
        eng.tensor_copy(MTM[:, 0:8], C_der[:, 8:16])
        return eng.tensor_copy(MTM[:, 8:16], C_modv[:, 24:32])
    E("vector", mk_rowstack, waits=[("v", v_ln3)], inc="v")
    v_rs = NOW("v")
    E("tensor", lambda eng: eng.matmul(pT0[0:16, :], MTM[:, 0:16], C_eye[:],
                                       is_transpose=True, start=True, stop=True),
      waits=[("v", v_rs)], inc="t")
    E("tensor", lambda eng: eng.matmul(pG[0:1, 0:1], C_eye[:, 0:1], C_eye[:, 0:1],
                                       start=True, stop=True), inc="t")
    t_rs = NOW("t")

    def mk_rowrep(eng):
        eng.tensor_copy(MTT[:], pT0[0:16, :])
        eng.drain()
        bc = MTT[:].unsqueeze(1).broadcast_to([16, KT, P])
        eng.tensor_tensor(RHSM[:, 0:C].rearrange("k (c j) -> k c j", j=P),
                          C_selrowa[:].rearrange("k (c j) -> k c j", j=P), bc, OP.mult)
        return eng.tensor_tensor(RHSM[:, C:2 * C].rearrange("k (c j) -> k c j", j=P),
                                 C_selrowb[:].rearrange("k (c j) -> k c j", j=P), bc, OP.mult)
    E("vector", mk_rowrep, waits=[("t", t_rs)], inc="v")
    v_rr = NOW("v")

    def mk_rowrep_mm(eng):
        inst = None
        for c in range(4):
            inst = eng.matmul(pG[:, c * 512:(c + 1) * 512], C_ones16[:],
                              RHSM[:, c * 512:(c + 1) * 512], start=True, stop=True)
        return inst
    E("tensor", mk_rowrep_mm, waits=[("v", v_rr)], inc="t")
    t_rr = NOW("t")
    E("scalar", lambda eng: (eng.activation(MUREP[:, 0:2 * C], pG[:, 0:2 * C], AF.Copy),
                             eng.drain())[1],
      waits=[("t", t_rr)], inc="s")
    s_rowrep = NOW("s")
    E("vector", None, waits=[("s", s_rowrep)], inc="v")
    _ckpt("ln2", [(MUREP[:], 0), (MTM[:], 128), (MTT[:], 256), (RHSM, 384)])

    wp_state = {"wgu": {}, "wd": {}}

    def wprep(nat, spill, nchunks, width, use_scr, start=0):
        ntile = width // P
        if use_scr:
            regs = (STRM[:, 0:512], STRM[:, 512:1024], STRM[:, 1024:1536],
                    STRM[:, 1536:2048], SCR[:, 4256:4768], SCR[:, 4768:5280])
            smalls = SCR[:, 2816:3232]
            stage0 = 3232
            state = wp_state["wd"]
        else:
            regs = (SCR[:, 0:512], SCR[:, 512:1024], SCR[:, 1024:1536],
                    SCR[:, 1536:2048], SCR[:, 3584:4096], SCR[:, 4096:4608])
            smalls = SCR[:, 2048:2432]
            stage0 = 2560
            state = wp_state["wgu"]

        def srcbuf(c2):
            if use_scr:
                return SCR[:, 0:width]
            return STRM[:, (c2 % 2) * C:(c2 % 2) * C + width]

        qi_load = {}
        t_trs = {}

        def emit_load(c2):
            if use_scr:
                wl = [("v", NOW("v")), ("s", s_rowrep), ("t", NOW("t"))]
            elif c2 < start + 2:
                wl = [("s", s_rowrep), ("v", NOW("v")), ("t", NOW("t"))]
            else:
                wl = [("t", t_trs[c2 - 2])]
            E("sync", (lambda n_, c3, sr: (lambda eng: eng.dma_start(
                out=sr, in_=n_[c3 * P:(c3 + 1) * P, :])))(nat, c2, srcbuf(c2)),
              waits=wl, inc="qi", inc_by=16)
            qi_load[c2] = NOW("qi")

        emit_load(start)
        for ch in range(start, nchunks):
            src_ = srcbuf(ch)
            if ch + 1 < nchunks and not use_scr:
                emit_load(ch + 1)
            wl_q = [("qi", qi_load[ch]), ("v", NOW("v"))]
            if ch == start:
                wl_q.append(("g", NOW("g")))
            qsem, qval = quant_emit(src_, width, regs, smalls, wl_q, state)
            g0 = 0
            while g0 < ntile:
                gn = min(8, ntile - g0)
                wl3 = [(qsem, qval), ("v", NOW("v"))]
                tg = tr_batch([src_[:, (g0 + i) * P:(g0 + i + 1) * P]
                               for i in range(gn)], wl3)
                E("vector", (lambda gn_, s0: (lambda eng: eng.tensor_copy(
                    SCR[:, s0:s0 + gn_ * P], pG[:, 0:gn_ * P])))(gn, stage0),
                  waits=[("t", tg)], inc="v")
                v_st = NOW("v")
                E("sync", (lambda c2, g_, gn_, s0: (lambda eng: eng.dma_start(
                    out=spill[c2, g_:g_ + gn_].rearrange("d p j -> p d j"),
                    in_=SCR[:, s0:s0 + gn_ * P].rearrange("p (d j) -> p d j", j=P))))(
                        ch, g0, gn, stage0),
                  waits=[("v", v_st)], inc="qi", inc_by=16)
                E("vector", None, waits=[("qi", NOW("qi"))], inc="v")
                g0 += gn
            t_trs[ch] = NOW("t")
            if use_scr and ch + 1 < nchunks:
                emit_load(ch + 1)

    wprep(wg_nat, wgspill, IKT, C, use_scr=False, start=WP_SCAN)
    wprep(wu_nat, wuspill, IKT, C, use_scr=False, start=WP_SCAN)
    wprep(wd_nat, wdspill, KT, I, use_scr=True)

    s_evh = None
    v_cout = None
    h1_state = {}
    h1_g = []
    for tt in range(NTT):
        SX = SCR[:, (tt % 2) * 1024:(tt % 2) * 1024 + 1024]
        wl1 = [("s", s_evh)] if tt >= 1 else [("v", NOW("v")), ("s", s_rowrep)]
        tg = tr_batch([ar1(k)[:, tt * P:(tt + 1) * P] for k in range(KT)], wl1)
        wl_ev = [("t", tg)]
        if tt >= 2:
            wl_ev.append(h1_g[tt - 2])
        E("scalar", (lambda t2, SX_: (lambda eng: (eng.activation(
            SX_, pG[:, 0:C], AF.Identity, scale=MTM[:, 32 + t2:33 + t2],
            bias=MTM[:, 48 + t2:49 + t2]), eng.drain())[1]))(tt, SX),
          waits=wl_ev, inc="s")
        s_evh = NOW("s")

        def mk_ln2(SX_):
            def body(eng):
                eng.tensor_tensor(SX_, SX_, MUREP[:, 0:C], OP.mult)
                return eng.tensor_tensor(SX_, SX_, MUREP[:, C:2 * C], OP.add)
            return body
        E("vector", mk_ln2(SX), waits=[("s", s_evh)], inc="v")
        h1_regs = (SCR[:, 2048:2560], SCR[:, 2560:3072], SCR[:, 3072:3584],
                   SCR[:, 3584:4096], SCR[:, 4480:4992], SCR[:, 4992:5504])
        wl_q = [("v", NOW("v"))]
        if tt == 0:
            wl_q.append(("g", NOW("g")))
        qsem, qval = quant_emit(SX, C, h1_regs, SCR[:, 4096:4480], wl_q, h1_state)
        h1_g.append((qsem, qval))
        wl2 = [(qsem, qval)] if tt == 0 else [(qsem, qval), ("v", v_cout)]
        tg2 = tr_batch([SX[:, k * P:(k + 1) * P] for k in range(KT)], wl2, off=KT)
        E("vector", (lambda t2: (lambda eng: eng.tensor_copy(
            AR2[:].rearrange("p (d t) -> p d t", t=T)[:, :, t2 * P:(t2 + 1) * P],
            pG[:, C:2 * C].rearrange("p (d j) -> p d j", j=P))))(tt),
          waits=[("t", tg2)], inc="v")
        v_cout = NOW("v")
    v_h1q = v_cout
    _ckpt("h1q", [(ar2(k), 128 * k) for k in range(KT)])

    CWS = [512, 512, 512, 512, 512, 256]
    tcch_hist = []
    steps = [(cch, Tc) for cch in range(6) for Tc in range(TC)]
    pend_q = None

    def s3a_tail(p):
        cch_, Tc_, cw_, ctn_, hb_, qw_ = p
        srcs = [SCR[:, hb_ + mt2 * cw_ + ct2 * P:hb_ + mt2 * cw_ + (ct2 + 1) * P]
                for mt2 in range(4) for ct2 in range(ctn_)]
        wl_tr = [qw_]
        if s3a_tail.v_prev is not None:
            wl_tr.append(("v", s3a_tail.v_prev))
        tg2 = tr_batch(srcs, wl_tr)
        E("vector", (lambda c_: (lambda eng: eng.tensor_copy(
            RREP[:, 0:c_ * 512].rearrange("p (ct mt j) -> p mt ct j", ct=c_, mt=4),
            pG[:, 0:4 * c_ * P].rearrange("p (mt ct j) -> p mt ct j", mt=4, ct=c_))))(ctn_),
          waits=[("t", tg2)], inc="v")
        v_stage = NOW("v")
        E("sync", (lambda c2, T2, c_: (lambda eng: eng.dma_start(
            out=hqspill[c2, T2][:, 0:c_ * 512], in_=RREP[:, 0:c_ * 512])))(cch_, Tc_, ctn_),
          waits=[("v", v_stage)], inc="qi", inc_by=16)
        E("vector", None, waits=[("qi", NOW("qi"))], inc="v")
        s3a_tail.v_prev = NOW("v")

    s3a_tail.v_prev = None
    s3a_vhm = [None]
    s3a_state = {}
    dd_map = {}

    def s3a_wloads(c3):
        cw3 = CWS[c3]
        for (spill, base) in [(wgspill, (c3 % 2) * 4096),
                              (wuspill, 8192 + (c3 % 2) * 4096)]:
            for d in range(KT):
                if c3 < 2:
                    wl = [("v", v_h1q), ("qo", qo_x2)]
                else:
                    wl = [("t", tcch_hist[c3 - 2])]
                E("gpsimd", (lambda sp, d2, c2, b_, cw2: (lambda eng: eng.dma_start(
                    out=AR1[:, b_ + d2 * 512:b_ + d2 * 512 + cw2]
                    .rearrange("p (c j) -> p c j", j=P),
                    in_=sp[4 * c2:4 * c2 + cw2 // P, d2]
                    .rearrange("c p j -> p c j"))))(spill, d, c3, base, cw3),
                  waits=wl, inc="qo", inc_by=16)
        dd_map[c3] = NOW("qo")

    s3a_wloads(0)
    for si, (cch, Tc) in enumerate(steps):
        cw = CWS[cch]
        ctiles = cw // P
        wgb = (cch % 2) * 4096
        wub = 8192 + (cch % 2) * 4096
        dd = dd_map[cch]
        if Tc == 1 and cch + 1 < 6:
            s3a_wloads(cch + 1)
        if Tc == 0:
            if cch == 0:
                for kt in range(16, 20):
                    E("sync", (lambda k2: (lambda eng: eng.dma_start(
                        out=STRM[:, (k2 - 16) * C:(k2 - 15) * C]
                        .rearrange("p (d j) -> p d j", j=P),
                        in_=wdspill[:, k2].rearrange("d p j -> p d j"))))(kt),
                      waits=[("v", NOW("v")), ("t", NOW("t"))], inc="qi", inc_by=16)
        hb = (si % 2) * 2048
        for mt in range(4):
            tok0 = Tc * 512 + mt * P

            def mk_gate(t0, cw2, b_):
                def body(eng):
                    inst = None
                    for k in range(KT):
                        inst = eng.matmul(pA[:, 0:cw2], ar2(k)[:, t0:t0 + P],
                                          AR1[:, b_ + k * 512:b_ + k * 512 + cw2],
                                          start=(k == 0), stop=(k == KT - 1))
                    return inst
                return body
            E("tensor", mk_gate(tok0, cw, wgb),
              waits=[("qo", dd), ("s", NOW("s"))], inc="t")
            tgate = NOW("t")
            wl_sil = [("t", tgate)]
            if s3a_vhm[0] is not None:
                wl_sil.append(("v", s3a_vhm[0]))
            E("scalar", (lambda cw2: (lambda eng: (eng.activation(
                MUREP[:, 0:cw2], pA[:, 0:cw2], AF.Silu),
                eng.drain())[1]))(cw),
              waits=wl_sil, inc="s")

            def mk_up(t0, cw2, b_):
                def body(eng):
                    inst = None
                    for k in range(KT):
                        inst = eng.matmul(pB[:, 0:cw2], ar2(k)[:, t0:t0 + P],
                                          AR1[:, b_ + k * 512:b_ + k * 512 + cw2],
                                          start=(k == 0), stop=(k == KT - 1))
                    return inst
                return body
            wl_up = [] if s3a_vhm[0] is None else [("v", s3a_vhm[0])]
            E("tensor", mk_up(tok0, cw, wub), waits=wl_up, inc="t")
            tup = NOW("t")
            E("vector", (lambda cw2, mt2, h_: (lambda eng: eng.tensor_tensor(
                SCR[:, h_ + mt2 * cw2:h_ + (mt2 + 1) * cw2], MUREP[:, 0:cw2],
                pB[:, 0:cw2], OP.mult)))(cw, mt, hb),
              waits=[("t", tup), ("s", NOW("s"))], inc="v")
            s3a_vhm[0] = NOW("v")
            if pend_q is not None and pend_q["done"] < pend_q["n"]:
                pend_q["ret"] = pend_q["emit"](pend_q["done"])
                pend_q["done"] += 1
        if Tc == TC - 1:
            tcch_hist.append(NOW("t"))
        vh = NOW("v")
        if pend_q is not None:
            while pend_q["done"] < pend_q["n"]:
                pend_q["ret"] = pend_q["emit"](pend_q["done"])
                pend_q["done"] += 1
            s3a_tail(pend_q["meta"] + (pend_q["ret"],))
        s3a_regs = (SCR[:, 4096:4608], SCR[:, 4608:5120], SCR[:, 5120:5632],
                    SCR[:, 5632:6144], MUREP[:, 512:1024], MUREP[:, 1024:1536])
        wl_q = [("v", vh)]
        if si == 0:
            wl_q.append(("g", NOW("g")))
        nq, eq = quant_chunks(SCR[:, hb:hb + 4 * cw], 4 * cw, s3a_regs,
                              MUREP[:, 1536:1920], wl_q, s3a_state, sign_dve=True)
        pend_q = {"emit": eq, "n": nq, "done": 0, "ret": None,
                  "meta": (cch, Tc, cw, ctiles, hb)}
    while pend_q["done"] < pend_q["n"]:
        pend_q["ret"] = pend_q["emit"](pend_q["done"])
        pend_q["done"] += 1
    s3a_tail(pend_q["meta"] + (pend_q["ret"],))
    qi_hq = NOW("qi")
    t_s3a = NOW("t")

    def wd_slice_ap(kt):
        if kt < 16:
            return AR1[:, kt * C:(kt + 1) * C]
        if kt < 20:
            return STRM[:, (kt - 16) * C:(kt - 15) * C]
        return SCR[:, 4096 + (kt - 20) * C:4096 + (kt - 19) * C]

    for kt in list(range(16)) + [20, 21]:
        wl = [("t", t_s3a), ("v", NOW("v"))]
        E("sync", (lambda k2: (lambda eng: eng.dma_start(
            out=wd_slice_ap(k2).rearrange("p (d j) -> p d j", j=P),
            in_=wdspill[:, k2].rearrange("d p j -> p d j"))))(kt),
          waits=wl, inc="qi", inc_by=16)
    qi_wd = NOW("qi")

    def x2slice(T2, m2):
        if T2 % 2 == 0:
            return SCR[:, m2 * 512:(m2 + 1) * 512]
        if m2 < 4:
            return RREP[:, m2 * 512:(m2 + 1) * 512]
        return MUREP[:, (m2 - 4) * 512:(m2 - 3) * 512]

    res_hist = []
    ttc_hist = []
    for Tc in range(TC):
        for kt in range(IKT):
            cc2, ct2 = kt // 4, kt % 4
            wl = [("qi", qi_hq), ("v", v_h1q)]
            if Tc > 0:
                wl.append(("t", ttc_hist[Tc - 1]))
            qeng, qs = ("sync", "qi") if kt % 2 == 0 else ("gpsimd", "qo")
            E(qeng, (lambda k2, T2, c3, t3: (lambda eng: eng.dma_start(
                out=AR2[:, k2 * 512:(k2 + 1) * 512],
                in_=hqspill[c3, T2][:, t3 * 512:(t3 + 1) * 512])))(kt, Tc, cc2, ct2),
              waits=wl, inc=qs, inc_by=16)
        ddo = NOW("qo")
        wl_x2 = [("v", NOW("v"))] if Tc < 2 else [("v", res_hist[8 * (Tc - 2) + 7])]
        if Tc % 2 == 0:
            E("sync", (lambda T2: (lambda eng: eng.dma_start(
                out=SCR[:, 0:KT * 512].rearrange("p (d j) -> p d j", j=512),
                in_=x2spill[:, T2 * 512:(T2 + 1) * 512]
                .rearrange("(d p) j -> p d j", p=P))))(Tc),
              waits=wl_x2, inc="qi", inc_by=16)
        else:
            E("sync", (lambda T2: (lambda eng: eng.dma_start(
                out=RREP[:, 0:4 * 512].rearrange("p (d j) -> p d j", j=512),
                in_=x2spill[0:512, T2 * 512:(T2 + 1) * 512]
                .rearrange("(d p) j -> p d j", p=P))))(Tc),
              waits=wl_x2, inc="qi", inc_by=16)
            E("sync", (lambda T2: (lambda eng: eng.dma_start(
                out=MUREP[:, 0:4 * 512].rearrange("p (d j) -> p d j", j=512),
                in_=x2spill[512:1024, T2 * 512:(T2 + 1) * 512]
                .rearrange("(d p) j -> p d j", p=P))))(Tc),
              waits=[("qi", NOW("qi"))], inc="qi", inc_by=16)
        dd = NOW("qi")
        for m in range(KT):
            ps = pAB[m % 2]
            j = Tc * KT + m

            def mk_down(m2, ps_):
                def body(eng):
                    inst = None
                    for kt in range(IKT):
                        inst = eng.matmul(ps_[:],
                                          wd_slice_ap(kt)[:, m2 * P:(m2 + 1) * P],
                                          AR2[:, kt * 512:(kt + 1) * 512],
                                          start=(kt == 0), stop=(kt == IKT - 1))
                    return inst
                return body
            wl_mm = [("qi", dd), ("qi", qi_wd), ("qo", ddo)]
            if j >= 2:
                wl_mm.append(("v", res_hist[j - 2]))
            E("tensor", mk_down(m, ps), waits=wl_mm, inc="t")
            td = NOW("t")
            E("vector", (lambda m2, ps_, T2: (lambda eng: eng.scalar_tensor_tensor(
                AR2[:, 11264 + m2 * 512:11264 + (m2 + 1) * 512], ps_[:],
                C_der[:, 24 + m2:25 + m2], x2slice(T2, m2),
                OP.mult, OP.add)))(m, ps, Tc), waits=[("t", td)], inc="v")
            res_hist.append(NOW("v"))
        ttc_hist.append(NOW("t"))
        v_out = NOW("v")
        E("gpsimd", (lambda T2: (lambda eng: eng.dma_start(
            out=outT[:, T2 * 512:(T2 + 1) * 512].rearrange("(d p) j -> p d j", p=P),
            in_=AR2[:, 11264:11264 + KT * 512].rearrange("p (d j) -> p d j", j=512))))(Tc),
          waits=[("v", v_out)], inc="qo", inc_by=16)
        E("vector", None, waits=[("qo", NOW("qo"))], inc="v")
    qo_fin = NOW("qo")

    E("gpsimd", None, waits=[("qo", qo_fin)])

    with (
        nc.semaphore("sem_qi") as h_qi,
        nc.semaphore("sem_qo") as h_qo,
        nc.semaphore("sem_t") as h_t,
        nc.semaphore("sem_v") as h_v,
        nc.semaphore("sem_s") as h_s,
        nc.semaphore("sem_g") as h_g,
        nc.Block() as block,
    ):
        sems = {"qi": h_qi, "qo": h_qo, "t": h_t, "v": h_v, "s": h_s, "g": h_g}

        def runner(name):
            def run(eng):
                for e, fn in pr.script:
                    if e == name:
                        fn(eng, sems)
            return run
        block.sync(runner("sync"))
        block.tensor(runner("tensor"))
        block.vector(runner("vector"))
        block.scalar(runner("scalar"))
        block.gpsimd(runner("gpsimd"))

    for g in reversed(guards):
        g.__exit__(None, None, None)
    return nc


def _host_layouts(ins):
    fp = np.float32

    def tile_fm(a):
        K, M = a.shape
        return np.ascontiguousarray(
            a.reshape(K // P, P, M).transpose(1, 0, 2).reshape(P, -1)).astype(fp)

    attn_W = np.asarray(ins["L0_attn_attn_W"], fp)
    proj_W = np.asarray(ins["L0_attn_proj_W"], fp)
    freq_W = np.asarray(ins["L0_attn_freq_W"], fp)
    attn_mk = np.ascontiguousarray(
        attn_W.reshape(KT, P, 16, P).transpose(2, 1, 0, 3)
        .reshape(16, P, KT * P).transpose(1, 0, 2).reshape(P, 16 * KT * P)).astype(fp)
    proj_mk = np.ascontiguousarray(
        proj_W.reshape(KT, P, KT, P).transpose(2, 1, 0, 3)
        .reshape(KT, P, KT * P).transpose(1, 0, 2).reshape(P, KT * KT * P)).astype(fp)
    d = dict(
        attn_mk=attn_mk,
        proj_mk=proj_mk,
        wqT_t=tile_fm(np.ascontiguousarray(attn_W[:, :C].T)),
        wfs_t=tile_fm(np.concatenate([freq_W[:C], freq_W[C:]], axis=1)),
        outw=np.ascontiguousarray(np.asarray(ins["L0_attn_out_W"], fp)),
        wmod_t=tile_fm(np.asarray(ins["L0_wmod"], fp)),
        wg_nat=np.ascontiguousarray(np.asarray(ins["L0_ffn_w_gate"], fp)),
        wu_nat=np.ascontiguousarray(np.asarray(ins["L0_ffn_w_up"], fp)),
        wd_nat=np.ascontiguousarray(np.asarray(ins["L0_ffn_w_down"], fp)),
        attb=np.ascontiguousarray(np.asarray(ins["L0_attn_attn_b"], fp).reshape(16, P).T),
        outb=np.ascontiguousarray(np.asarray(ins["L0_attn_out_b"], fp).reshape(KT, P).T),
        projb=np.ascontiguousarray(np.asarray(ins["L0_attn_proj_b"], fp).reshape(KT, P).T),
        fbcol=np.ascontiguousarray(np.asarray(ins["L0_attn_freq_b"], fp).reshape(NF, 1)),
        idcol=np.ascontiguousarray(np.asarray(ins["L0_attn_identity"], fp).reshape(KT, P).T),
        eye=np.eye(P, dtype=fp),
    )
    import ml_dtypes
    bf = ml_dtypes.bfloat16
    selm16 = np.zeros((16, T), np.float32)
    for c in range(NTT):
        selm16[c, c * P:(c + 1) * P] = 1.0
    selrowa = np.zeros((16, C), np.float32)
    selrowb = np.zeros((16, C), np.float32)
    for c in range(KT):
        selrowa[c, c * P:(c + 1) * P] = 1.0
        selrowb[8 + c, c * P:(c + 1) * P] = 1.0
    d["selm16"] = selm16.astype(bf)
    d["selrowa"] = selrowa.astype(bf)
    d["selrowb"] = selrowb.astype(bf)
    return d


def kernel(**inputs):
    x = np.ascontiguousarray(np.asarray(inputs["L0_attn_x"], np.float32))
    shared = _host_layouts(inputs)
    c_in = np.asarray(inputs["L0_c"], np.float32)

    nc = build_program()
    in_maps = []
    for core in range(8):
        b, p = core // 2, core % 2
        m = dict(shared)
        m["xT"] = np.ascontiguousarray(x[b, p::2, :].T)
        if p == 0:
            xprev = np.concatenate([np.zeros((1, C), np.float32),
                                    x[b, 1::2, :][:T - 1]], axis=0)
            ev = 1.0
        else:
            xprev = x[b, 0::2, :]
            ev = 0.0
        m["xprevT"] = np.ascontiguousarray(xprev.T)
        m["c_col"] = np.ascontiguousarray(c_in[b].reshape(KT, P).T)
        m["evenm"] = np.full((NF, 1), ev, np.float32)
        in_maps.append(m)

    res = run_bass_kernel_spmd(nc, in_maps, core_ids=list(range(8)))
    out = np.empty_like(x)
    for core in range(8):
        b, p = core // 2, core % 2
        out[b, p::2, :] = res.results[core]["outT"].T
    return out
